# version 01
# baseline (speedup 1.0000x reference)
import numpy as np
import ml_dtypes
from contextlib import ExitStack

import concourse.bass as bass
import concourse.mybir as mybir
from concourse.bass_utils import run_bass_kernel_spmd

F32 = mybir.dt.float32
BF16 = mybir.dt.bfloat16
I32 = mybir.dt.int32
AF = mybir.ActivationFunctionType
ALU = mybir.AluOpType
AX = mybir.AxisListType

B, T, D = 16, 2048, 1024
NCORE = 8
BPC = B // NCORE
CH = D // 128
NH, HD = 16, 64
TBLK = 512
NTB = T // TBLK
NTT = T // 128

ENGS = ("sync", "vector", "scalar", "tensor", "gpsimd")


class Sem:
    def __init__(self, nc, stack, name):
        self.h = stack.enter_context(nc.semaphore(name))
        self.n = 0


class Buf:
    __slots__ = ("w", "r")

    def __init__(self):
        self.w = None
        self.r = []


class Prog:
    def __init__(self, nc, stack):
        self.nc = nc
        self.stack = stack
        self.q = {e: [] for e in ENGS}
        self.sem = {k: Sem(nc, stack, "sem_" + k) for k in
                    ("t", "v", "s", "g", "ds", "dg", "cc")}
        self.engsem = {"tensor": self.sem["t"], "vector": self.sem["v"],
                       "scalar": self.sem["s"], "gpsimd": self.sem["g"]}
        self.bufs = {}

    def buf(self, obj):
        b = self.bufs.get(id(obj))
        if b is None:
            b = Buf()
            self.bufs[id(obj)] = b
        return b

    def _emit(self, eng, build, deps, sem, inck):
        sem.n += inck
        tok = (sem, sem.n, eng)
        dd = {}
        for d in deps:
            if d is None:
                continue
            s, v, e = d
            if (e == eng == "tensor") and s is self.engsem.get(eng):
                continue
            if s not in dd or dd[s] < v:
                dd[s] = v
        dl = [(s.h, v) for s, v in dd.items()]

        def run(h, dl=dl, build=build, sh=sem.h, inck=inck):
            for (x, v) in dl:
                h.wait_ge(x, v)
            build(h).then_inc(sh, inck)
        self.q[eng].append(run)
        return tok

    def do(self, eng, build, reads=(), writes=(), deps=(), dmaq=None):
        deps = list(deps)
        rb = [self.buf(x) for x in reads]
        wb = [self.buf(x) for x in writes]
        for b_ in rb:
            if b_.w is not None:
                deps.append(b_.w)
        for b_ in wb:
            if b_.w is not None:
                deps.append(b_.w)
            deps.extend(b_.r)
        if dmaq is not None:
            sem, inck = self.sem["ds" if dmaq in ("sync", "scalar") else "dg"], 16
        else:
            sem, inck = self.engsem[eng], 1
        tok = self._emit(eng, build, deps, sem, inck)
        for b_ in rb:
            b_.r.append(tok)
        for b_ in wb:
            b_.w = tok
            b_.r = []
        return tok

    def barrier_token_all(self):
        return [(s, s.n, None) for s in self.sem.values()]

    def barrier(self):
        toks = self.barrier_token_all()
        for e in ENGS:
            self.wait(e, toks)

    def wait(self, eng, deps):
        dd = {}
        for d in deps:
            if d is None:
                continue
            s, v, _ = d
            if s not in dd or dd[s] < v:
                dd[s] = v
        dl = [(s.h, v) for s, v in dd.items()]

        def run(h, dl=dl):
            for (x, v) in dl:
                h.wait_ge(x, v)
        self.q[eng].append(run)

    def finalize(self, block):
        for e in ENGS:
            ops = self.q[e]
            if not ops:
                continue

            def mk(ops):
                def body(h):
                    for f in ops:
                        f(h)
                return body
            getattr(block, e)(mk(ops))

    def dma(self, q, out, in_, reads=(), writes=(), deps=()):
        return self.do(q, lambda h, o=out, i=in_: h.dma_start(out=o, in_=i),
                       reads=reads, writes=writes, deps=deps, dmaq=q)

    def dmat(self, out, in_, reads=(), writes=(), deps=()):
        return self.do("sync", lambda h, o=out, i=in_: h.dma_start_transpose(o, i),
                       reads=reads, writes=writes, deps=deps, dmaq="sync")

    def mm(self, out, lhsT, rhs, reads=(), writes=(), deps=(), start=True, stop=True):
        return self.do("tensor",
                       lambda h, o=out, l=lhsT, r=rhs, a=start, b=stop:
                       h.matmul(o, l, r, start=a, stop=b),
                       reads=reads, writes=writes, deps=deps)

    def act(self, out, in_, func=AF.Identity, bias=None, scale=None, accum_out=None,
            reads=(), writes=(), deps=()):
        kw = {}
        if bias is not None:
            kw["bias"] = bias
        if scale is not None:
            kw["scale"] = scale
        if accum_out is not None:
            kw["accum_out"] = accum_out
        return self.do("scalar",
                       lambda h, o=out, i=in_, f=func, kw=kw:
                       h.activation(out=o, in_=i, func=f, **kw),
                       reads=reads, writes=writes, deps=deps)

    def ts(self, out, in0, s1, op0, s2=None, op1=None, reads=(), writes=(), deps=()):
        return self.do("vector",
                       lambda h, o=out, i=in0, a=s1, b=s2, p0=op0, p1=op1:
                       h.tensor_scalar(out=o, in0=i, scalar1=a, scalar2=b, op0=p0,
                                       **({"op1": p1} if p1 is not None else {})),
                       reads=reads, writes=writes, deps=deps)

    def tt(self, out, in0, in1, op, reads=(), writes=(), deps=()):
        return self.do("vector",
                       lambda h, o=out, a=in0, b=in1, p=op:
                       h.tensor_tensor(out=o, in0=a, in1=b, op=p),
                       reads=reads, writes=writes, deps=deps)

    def stt(self, out, in0, scalar, in1, op0, op1, reads=(), writes=(), deps=()):
        return self.do("vector",
                       lambda h, o=out, a=in0, s=scalar, b=in1, p0=op0, p1=op1:
                       h.scalar_tensor_tensor(out=o, in0=a, scalar=s, in1=b, op0=p0, op1=p1),
                       reads=reads, writes=writes, deps=deps)

    def vcopy(self, out, in_, reads=(), writes=(), deps=()):
        return self.do("vector", lambda h, o=out, i=in_: h.tensor_copy(out=o, in_=i),
                       reads=reads, writes=writes, deps=deps)

    def vred(self, out, in_, op=ALU.max, absval=True, reads=(), writes=(), deps=()):
        return self.do("vector",
                       lambda h, o=out, i=in_, p=op, a=absval:
                       h.tensor_reduce(out=o, in_=i, axis=AX.X, op=p,
                                       apply_absolute_value=a),
                       reads=reads, writes=writes, deps=deps)

    def vrecip(self, out, in_, reads=(), writes=(), deps=()):
        return self.do("vector", lambda h, o=out, i=in_: h.reciprocal(out=o, in_=i),
                       reads=reads, writes=writes, deps=deps)

    def memset(self, ap, c, writes=(), deps=(), eng="vector"):
        return self.do(eng, lambda h, a=ap, v=c: h.memset(a, v),
                       writes=writes, deps=deps)

    def gredc(self, out, in_, op=ALU.max, reads=(), writes=(), deps=()):
        return self.do("gpsimd",
                       lambda h, o=out, i=in_, p=op:
                       h.tensor_reduce(out=o, in_=i, axis=AX.C, op=p,
                                       apply_absolute_value=False),
                       reads=reads, writes=writes, deps=deps)

    def allreduce_max(self, cin, cout, sb, reads=(), writes=()):
        st = self.dma("gpsimd", cin.ap(), sb[:, :], reads=[sb])
        ar = self.do("gpsimd",
                     lambda h: h.collective_compute(
                         "AllReduce", ALU.max, replica_groups=[list(range(NCORE))],
                         ins=[cin.ap().opt()], outs=[cout.ap().opt()]),
                     deps=[st])
        return self.dma("gpsimd", sb[:, :], cout.ap(), writes=[sb], deps=[ar])


def SB(nc, stack, name, shape, dtype):
    return stack.enter_context(nc.sbuf_tensor(name, shape, dtype))


def PS(nc, stack, name, shape, dtype=F32):
    return stack.enter_context(nc.psum_tensor(name, shape, dtype))


_CONSTS = {}


def host_consts():
    if _CONSTS:
        return _CONSTS
    bf = ml_dtypes.bfloat16
    m = np.zeros((128, 4, 4, 128), np.float32)
    for r in range(4):
        for j in range(4):
            if j > r:
                m[:, r, j, :] = 1.0
            elif j == r:
                m[:, r, j, :] = np.triu(np.ones((128, 128), np.float32))
    _CONSTS["cmask"] = np.ascontiguousarray(m.reshape(128, 4 * 512)).astype(bf)
    inv = 1.0 / 10000.0 ** (np.arange(32, dtype=np.float32) / 32.0)
    ang = np.arange(T, dtype=np.float32)[:, None] * inv[None, :]
    _CONSTS["rope_sin"] = np.ascontiguousarray(np.sin(ang).T).astype(np.float32)
    _CONSTS["rope_cos"] = np.ascontiguousarray(np.cos(ang).T).astype(np.float32)
    import jax
    with jax.default_device(jax.devices("cpu")[0]):
        g = np.asarray(jax.random.gumbel(jax.random.key(42), (B, T, 8))).astype(np.float32) * 0.05
    _CONSTS["gumbel"] = np.ascontiguousarray(g.transpose(0, 2, 1))
    return _CONSTS


def build_l0a(debug=False):
    nc = bass.Bass("TRN2", target_bir_lowering=False, debug=False)
    dr = {}
    dr["x"] = nc.dram_tensor("x", [BPC, D, T], F32, kind="ExternalInput").ap()
    for w in ("wq", "wk", "wv", "wo"):
        dr[w] = nc.dram_tensor(w, [D, D], F32, kind="ExternalInput").ap()
    for b_ in ("bq", "bk", "bv", "bo"):
        dr[b_] = nc.dram_tensor(b_, [1, D], F32, kind="ExternalInput").ap()
    dr["cmask"] = nc.dram_tensor("cmask", [128, 4 * 512], F32, kind="ExternalInput").ap()
    dr["xo"] = nc.dram_tensor("xo", [BPC, D, T], F32, kind="ExternalOutput").ap()
    dr["wscr"] = nc.dram_tensor("wscr", [D, D], BF16)
    dr["pscr"] = nc.dram_tensor("pscr", [1, 128], F32)
    dr["oscr"] = nc.dram_tensor("oscr", [BPC, D, T], F32,
                                kind="ExternalOutput" if debug else "Internal")
    dr["cc_in"] = nc.dram_tensor("cc_in", [1, 16], F32)
    dr["cc_out"] = nc.dram_tensor("cc_out", [1, 16], F32)
    if debug:
        for nm, shp in (("d_scal2", [1, 32]), ("d_yq", [128, CH * T]),
                        ("d_rt", [128, 1]), ("d_rw", [1, 128]),
                        ("d_scal", [1, 32]), ("d_row", [33, BPC * T]),
                        ("d_xq", [128, CH * T]), ("d_qT", [64, T]),
                        ("d_kT", [64, T]), ("d_vh", [128, NTT * 65]),
                        ("d_negm", [1, NH])):
            dr[nm] = nc.dram_tensor(nm, shp, F32, kind="ExternalOutput").ap()
    with ExitStack() as stack:
        p = Prog(nc, stack)
        emit_l0a(nc, stack, p, dr, debug=debug)
        p.wait("sync", p.barrier_token_all())
        with nc.Block() as blk:
            p.finalize(blk)
    return nc


def emit_l0a(nc, stack, p, dr, debug=False):
    A = lambda n, s, d: SB(nc, stack, "a_" + n, s, d)
    wstage = A("wstage", [128, CH * D], F32)
    WT = {w: A("WT" + w, [128, CH * D], BF16) for w in ("q", "k", "v", "o")}
    xblk = wstage
    xqT = A("xqT", [128, CH * T], BF16)
    qT = A("qT", [64, T], BF16)
    kT = A("kT", [64, T], BF16)
    vh = A("vh", [128, NTT * 65], BF16)
    wi32 = A("wi32", [128, D], I32)
    scr = A("scr", [128, TBLK], F32)
    scr2 = A("scr2", [128, TBLK], F32)
    scr3 = A("scr3", [128, TBLK], F32)
    sq64 = A("sq64", [64, TBLK], F32)
    pt = [A("pt0", [128, TBLK], BF16), A("pt1", [128, TBLK], BF16)]
    cmask = A("cmask", [128, 4 * 512], BF16)
    rowpk = A("rowpk", [33, BPC * T], F32)
    dnh = A("dnh", [1, TBLK], F32)
    stat = A("stat", [1, TBLK], F32)
    stat2 = A("stat2", [1, TBLK], F32)
    stat3 = A("stat3", [1, TBLK], F32)
    brawp = A("brawp", [65, D], F32)
    brawo = A("brawo", [1, D], F32)
    bscp = A("bscp", [65, D], BF16)
    bsco = A("bsco", [1, D], BF16)
    smax = A("smax", [128, 1], F32)
    rtmp = A("rtmp", [128, 1], F32)
    sc16 = A("sc16", [1, 16], F32)
    scal = A("scal", [1, 32], F32)
    scb = A("scb", [128, 8], F32)
    negm = A("negm", [1, NH], F32)
    negmb = A("negmb", [128, NH], F32)
    onesfp = A("onesfp", [65, 128], F32)
    rowtmp = A("rowtmp", [1, 128], F32)
    onesfc = A("onesfc", [128, 1], F32)
    onesbp = A("onesbp", [65, TBLK], BF16)
    ones64 = A("ones64", [64, 1], F32)
    ps_s = [PS(nc, stack, "ps_s0", [128, TBLK]), PS(nc, stack, "ps_s1", [128, TBLK])]
    ps_o = [PS(nc, stack, "ps_o0", [65, TBLK]), PS(nc, stack, "ps_o1", [65, TBLK])]
    ps_w = PS(nc, stack, "ps_w", [128, TBLK])
    ps_w2 = PS(nc, stack, "ps_w2", [128, TBLK])
    ps_r = PS(nc, stack, "ps_r", [1, TBLK])

    BROW = {"q": brawp[0:1, :], "k": brawp[32:33, :], "v": brawp[64:65, :], "o": brawo[0:1, :]}
    BSC = {"q": bscp[0:1, :], "k": bscp[32:33, :], "v": bscp[64:65, :], "o": bsco[0:1, :]}
    BRB = {"q": brawp, "k": brawp, "v": brawp, "o": brawo}
    BSB = {"q": bscp, "k": bscp, "v": bscp, "o": bsco}

    p.memset(onesfp[:], 1.0, writes=[onesfp])
    p.memset(onesfc[:], 1.0, writes=[onesfc])
    p.memset(onesbp[:], 1.0, writes=[onesbp])
    p.memset(ones64[:], 1.0, writes=[ones64])
    p.dma("gpsimd", cmask[:], dr["cmask"], writes=[cmask])
    for k in ("q", "k", "v", "o"):
        p.dma("sync", BROW[k], dr["b" + k], writes=[BRB[k]])

    xfm, xofm = dr["x"], dr["xo"]
    xb = lambda b, c, t0, t1: xfm[b, c * 128:(c + 1) * 128, t0:t1]

    def xblk_load(b, T0):
        return p.dma(
            "sync", xblk[:, 0:CH * TBLK].rearrange("p (c t) -> p c t", c=CH),
            xfm[b].rearrange("(c p) t -> p c t", p=128)[:, :, T0:T0 + TBLK],
            writes=[xblk])

    def bcast_scalar(src_ap, dst_col):
        p.mm(ps_w2[:, dst_col:dst_col + 1], onesfp[0:1, :], src_ap,
             reads=[onesfp, scal], writes=[ps_w2])
        return p.act(scb[:, dst_col:dst_col + 1], ps_w2[:, dst_col:dst_col + 1],
                     reads=[ps_w2], writes=[scb])

    def pmax_col(dst_ap, col, colbuf, dstbuf):
        p.dma("sync", dr["pscr"].ap().rearrange("a b -> (a b)"), col, reads=[colbuf])
        p.dma("sync", rowtmp[0:1, :], dr["pscr"].ap(), writes=[rowtmp],
              deps=[(p.sem["ds"], p.sem["ds"].n, None)])
        p.vred(dst_ap, rowtmp[0:1, 0:128], ALU.max, False, reads=[rowtmp], writes=[dstbuf])

    def quant_scales(dst_s, dst_sinv, src_ap, srcbuf):
        p.ts(scal[:, 30:31], src_ap, 127e-8, ALU.max, reads=[srcbuf], writes=[scal])
        p.ts(scal[:, dst_s:dst_s + 1], scal[:, 30:31], 1.0 / 127.0, ALU.mult,
             reads=[scal], writes=[scal])
        p.vrecip(scal[:, 31:32], scal[:, 30:31], reads=[scal], writes=[scal])
        p.ts(scal[:, dst_sinv:dst_sinv + 1], scal[:, 31:32], 127.0, ALU.mult,
             reads=[scal], writes=[scal])

    for wi, w in enumerate(("q", "k", "v", "o")):
        p.dma("sync", wstage[:, :].rearrange("p (c o) -> p c o", c=CH),
              dr["w" + w].rearrange("(c p) o -> p c o", p=128),
              writes=[wstage])
        p.vred(rtmp[:], wstage[:, :], ALU.max, True, reads=[wstage], writes=[rtmp])
        if debug and wi == 0:
            p.dma("gpsimd", dr["d_rt"], rtmp[:], reads=[rtmp])
        pmax_col(scal[:, 8:9], rtmp[:], rtmp, scal)
        if debug and wi == 0:
            p.dma("gpsimd", dr["d_rw"], rowtmp[0:1, :], reads=[rowtmp])
        quant_scales(2 * wi, 2 * wi + 1, scal[:, 8:9], scal)
        bcast_scalar(scal[:, 2 * wi + 1:2 * wi + 2], 0)
        for j in range(CH):
            p.ts(wi32[:, :], wstage[:, j * D:(j + 1) * D], scb[:, 0:1], ALU.mult,
                 reads=[wstage, scb], writes=[wi32])
            p.vcopy(WT[w][:, j * D:(j + 1) * D], wi32[:, :], reads=[wi32], writes=[WT[w]])
        p.dma("gpsimd", dr["wscr"].rearrange("(c p) o -> p c o", p=128),
              WT[w][:, :].rearrange("p (c o) -> p c o", c=CH),
              reads=[WT[w]], writes=[dr["wscr"]])
        for c in range(CH):
            p.dmat(WT[w][:, c * D:(c + 1) * D], dr["wscr"][:, c * 128:(c + 1) * 128],
                   reads=[dr["wscr"]],
                   writes=[WT[w]] if c == CH - 1 else [],
                   deps=[(p.sem["dg"], p.sem["dg"].n, None)])

    p.memset(smax[:], 0.0, writes=[smax])
    for b in range(BPC):
        for tb in range(NTB):
            T0 = tb * TBLK
            R0 = b * T + T0
            xblk_load(b, T0)
            for c in range(CH):
                p.mm(ps_r[:, :], onesfc[:, :], xblk[:, c * TBLK:(c + 1) * TBLK],
                     reads=[onesfc, xblk], writes=[ps_r],
                     start=(c == 0), stop=(c == CH - 1))
            p.ts(stat[:, :], ps_r[0:1, :], 1.0 / D, ALU.mult, reads=[ps_r], writes=[stat])
            for c in range(CH):
                p.act(scr[:, :], xblk[:, c * TBLK:(c + 1) * TBLK], AF.Square,
                      reads=[xblk], writes=[scr])
                p.mm(ps_r[:, :], onesfc[:, :], scr[:, :], reads=[onesfc, scr],
                     writes=[ps_r], start=(c == 0), stop=(c == CH - 1))
            p.ts(stat2[:, :], ps_r[0:1, :], 1.0 / D, ALU.mult, reads=[ps_r], writes=[stat2])
            p.act(stat3[:, :], stat[:, :], AF.Square, reads=[stat], writes=[stat3])
            p.tt(stat2[:, :], stat2[:, :], stat3[:, :], ALU.subtract,
                 reads=[stat3], writes=[stat2])
            p.ts(stat2[:, :], stat2[:, :], 1e-5, ALU.add, reads=[stat2], writes=[stat2])
            p.act(stat3[:, :], stat2[:, :], AF.Sqrt, reads=[stat2], writes=[stat3])
            p.vrecip(rowpk[0:1, R0:R0 + TBLK], stat3[:, :], reads=[stat3], writes=[rowpk])
            p.tt(stat[:, :], stat[:, :], rowpk[0:1, R0:R0 + TBLK], ALU.mult,
                 reads=[rowpk], writes=[stat])
            p.ts(rowpk[32:33, R0:R0 + TBLK], stat[:, :], -1.0, ALU.mult,
                 reads=[stat], writes=[rowpk])
            p.mm(ps_w[:, :], onesfp[0:1, :], rowpk[0:1, R0:R0 + TBLK],
                 reads=[onesfp, rowpk], writes=[ps_w])
            p.act(scr[:, :], ps_w[:, :], reads=[ps_w], writes=[scr])
            p.mm(ps_w2[:, :], onesfp[32:33, :], rowpk[32:33, R0:R0 + TBLK],
                 reads=[onesfp, rowpk], writes=[ps_w2])
            p.act(scr2[:, :], ps_w2[:, :], reads=[ps_w2], writes=[scr2])
            for c in range(CH):
                p.tt(scr3[:, :], xblk[:, c * TBLK:(c + 1) * TBLK], scr[:, :],
                     ALU.mult, reads=[xblk, scr], writes=[scr3])
                p.tt(scr3[:, :], scr3[:, :], scr2[:, :], ALU.add,
                     reads=[scr2], writes=[scr3])
                p.vred(rtmp[:], scr3[:, :], ALU.max, True, reads=[scr3], writes=[rtmp])
                p.tt(smax[:], smax[:], rtmp[:], ALU.max, reads=[rtmp], writes=[smax])
    pmax_col(sc16[:, 0:1], smax[:], smax, sc16)
    p.allreduce_max(dr["cc_in"], dr["cc_out"], sc16)
    quant_scales(16, 17, sc16[:, 0:1], sc16)
    bcast_scalar(scal[:, 17:18], 3)
    for wi, w in enumerate(("q", "k", "v")):
        p.tt(scal[:, 20 + wi:21 + wi], scal[:, 16:17], scal[:, 2 * wi:2 * wi + 1],
             ALU.mult, reads=[scal], writes=[scal])
        if w == "q":
            p.ts(scal[:, 25:26], scal[:, 20 + wi:21 + wi], 0.125, ALU.mult,
                 reads=[scal], writes=[scal])
            bcast_scalar(scal[:, 25:26], 4 + wi)
        else:
            bcast_scalar(scal[:, 20 + wi:21 + wi], 4 + wi)
        p.vrecip(scal[:, 11:12], scal[:, 20 + wi:21 + wi], reads=[scal], writes=[scal])
        p.ts(BSC[w], BROW[w], scal[:, 11:12], ALU.mult,
             reads=[BRB[w], scal], writes=[BSB[w]])

    p.memset(smax[:], 0.0, writes=[smax])
    for b in range(BPC):
        for tb in range(NTB):
            T0 = tb * TBLK
            R0 = b * T + T0
            p.ts(stat[:, :], rowpk[0:1, R0:R0 + TBLK], scal[:, 17:18], ALU.mult,
                 reads=[rowpk, scal], writes=[stat])
            p.ts(stat2[:, :], rowpk[32:33, R0:R0 + TBLK], scal[:, 17:18], ALU.mult,
                 reads=[rowpk, scal], writes=[stat2])
            p.mm(ps_w[:, :], onesfp[0:1, :], stat[0:1, :], reads=[onesfp, stat], writes=[ps_w])
            p.act(scr[:, :], ps_w[:, :], reads=[ps_w], writes=[scr])
            p.mm(ps_w2[:, :], onesfp[0:1, :], stat2[0:1, :], reads=[onesfp, stat2], writes=[ps_w2])
            p.act(scr2[:, :], ps_w2[:, :], reads=[ps_w2], writes=[scr2])
            xblk_load(b, T0)
            for c in range(CH):
                p.tt(scr3[:, :], xblk[:, c * TBLK:(c + 1) * TBLK], scr[:, :],
                     ALU.mult, reads=[xblk, scr], writes=[scr3])
                p.tt(wi32[:, 0:TBLK], scr3[:, :], scr2[:, :], ALU.add,
                     reads=[scr3, scr2], writes=[wi32])
                p.vcopy(xqT[:, c * T + T0:c * T + T0 + TBLK], wi32[:, 0:TBLK],
                        reads=[wi32], writes=[xqT])
        if debug and b == 0:
            p.dma("gpsimd", dr["d_xq"], xqT[:, :], reads=[xqT])
            p.dma("gpsimd", dr["d_scal"], scal[:, :], reads=[scal])
            p.dma("gpsimd", dr["d_row"], rowpk[:, :], reads=[rowpk])
        p.memset(vh[:, :].rearrange("p (n k) -> p n k", k=65)[:, :, 64:65], 1.0,
                 writes=[vh])
        for h in range(NH):
            o0 = h * HD
            for dst, w, ci in ((qT, "q", 4), (kT, "k", 5)):
                for tb in range(NTB):
                    T0 = tb * TBLK
                    for c in range(CH):
                        p.mm(ps_w[0:64, :], WT[w][:, c * D + o0:c * D + o0 + HD],
                             xqT[:, c * T + T0:c * T + T0 + TBLK],
                             reads=[WT[w], xqT], writes=[ps_w],
                             start=(c == 0), stop=False)
                    brow_ = 0 if w == "q" else 32
                    p.mm(ps_w[0:64, :], BSC[w][0:1, o0:o0 + HD],
                         onesbp[brow_:brow_ + 1, :],
                         reads=[BSB[w], onesbp], writes=[ps_w], start=False, stop=True)
                    p.act(dst[:, T0:T0 + TBLK], ps_w[0:64, :], AF.Identity,
                          scale=scb[0:64, ci:ci + 1], reads=[ps_w, scb], writes=[dst])
            for tt_ in range(NTT):
                t0 = tt_ * 128
                for c in range(CH):
                    p.mm(ps_w[:, 0:64], xqT[:, c * T + t0:c * T + t0 + 128],
                         WT["v"][:, c * D + o0:c * D + o0 + HD],
                         reads=[WT["v"], xqT], writes=[ps_w],
                         start=(c == 0), stop=False)
                p.mm(ps_w[:, 0:64], onesbp[64:65, 0:128], BSC["v"][0:1, o0:o0 + HD],
                     reads=[onesbp, BSB["v"]], writes=[ps_w], start=False, stop=True)
                p.act(vh[:, tt_ * 65:tt_ * 65 + 64], ps_w[:, 0:64], AF.Identity,
                      scale=scb[:, 6:7], reads=[ps_w, scb], writes=[vh])
            for qi, src_ in enumerate((qT, kT)):
                col = 26 + qi
                for tb in range(NTB):
                    T0 = tb * TBLK
                    p.act(sq64[:, :], src_[:, T0:T0 + TBLK], AF.Square,
                          reads=[src_], writes=[sq64])
                    p.mm(ps_r[:, :], ones64[:, :], sq64[:, :],
                         reads=[ones64, sq64], writes=[ps_r])
                    p.vred(stat[:, 0:1], ps_r[0:1, :], ALU.max, False,
                           reads=[ps_r], writes=[stat])
                    if tb == 0:
                        p.vcopy(scal[:, col:col + 1], stat[:, 0:1], reads=[stat], writes=[scal])
                    else:
                        p.tt(scal[:, col:col + 1], scal[:, col:col + 1], stat[:, 0:1],
                             ALU.max, reads=[stat], writes=[scal])
            p.tt(scal[:, 28:29], scal[:, 26:27], scal[:, 27:28], ALU.mult,
                 reads=[scal], writes=[scal])
            p.act(scal[:, 29:30], scal[:, 28:29], AF.Sqrt, reads=[scal], writes=[scal])
            p.ts(negm[:, h:h + 1], scal[:, 29:30], -1.0, ALU.mult,
                 reads=[scal], writes=[negm])
            p.mm(ps_w2[:, 8:9], onesfp[0:1, :], negm[0:1, h:h + 1],
                 reads=[onesfp, negm], writes=[ps_w2])
            p.act(negmb[:, h:h + 1], ps_w2[:, 8:9], reads=[ps_w2], writes=[negmb])
            if debug and b == 0 and h == 0:
                p.dma("gpsimd", dr["d_qT"], qT[:, :], reads=[qT])
                p.dma("gpsimd", dr["d_kT"], kT[:, :], reads=[kT])
                p.dma("gpsimd", dr["d_vh"], vh[:, :], reads=[vh])
                p.dma("gpsimd", dr["d_negm"], negm[:, :], reads=[negm])
            for j0 in range(NTB):
                T0 = j0 * TBLK
                na = 4 * j0 + 4
                po = ps_o[j0 % 2]
                for a in range(na):
                    s_ = ps_s[a % 2]
                    pe = pt[a % 2]
                    p.mm(s_[:, :], kT[:, a * 128:(a + 1) * 128], qT[:, T0:T0 + TBLK],
                         reads=[kT, qT], writes=[s_])
                    p.act(pe[:, :], s_[:, :], AF.Exp, bias=negmb[:, h:h + 1],
                          reads=[s_, negmb], writes=[pe])
                    r = a - 4 * j0
                    if r >= 0:
                        p.tt(pe[:, :], pe[:, :], cmask[:, r * 512:(r + 1) * 512],
                             ALU.mult, reads=[cmask], writes=[pe])
                    p.mm(po[:, :], vh[:, a * 65:a * 65 + 65], pe[:, :],
                         reads=[vh, pe], writes=[po],
                         start=(a == 0), stop=(a == na - 1))
                p.vrecip(dnh[:, :], po[64:65, :], reads=[po], writes=[dnh])
                p.mm(ps_w[0:64, :], onesfp[0:1, 0:64], dnh[0:1, :],
                     reads=[onesfp, dnh], writes=[ps_w])
                p.act(scr[0:64, :], ps_w[0:64, :], reads=[ps_w], writes=[scr])
                p.tt(scr2[0:64, :], po[0:64, :], scr[0:64, :], ALU.mult,
                     reads=[po, scr], writes=[scr2])
                p.vred(rtmp[0:64, :], scr2[0:64, :], ALU.max, True,
                       reads=[scr2], writes=[rtmp])
                p.tt(smax[0:64, :], smax[0:64, :], rtmp[0:64, :], ALU.max,
                     reads=[rtmp], writes=[smax])
                p.dma("sync", dr["oscr"][b, o0:o0 + HD, T0:T0 + TBLK], scr2[0:64, :],
                      reads=[scr2])

    pmax_col(sc16[:, 0:1], smax[:], smax, sc16)
    p.allreduce_max(dr["cc_in"], dr["cc_out"], sc16)
    quant_scales(16, 17, sc16[:, 0:1], sc16)
    bcast_scalar(scal[:, 17:18], 3)
    p.tt(scal[:, 23:24], scal[:, 16:17], scal[:, 6:7], ALU.mult,
         reads=[scal], writes=[scal])
    bcast_scalar(scal[:, 23:24], 7)
    p.vrecip(scal[:, 24:25], scal[:, 23:24], reads=[scal], writes=[scal])
    p.ts(BSC["o"], BROW["o"], scal[:, 24:25], ALU.mult,
         reads=[BRB["o"], scal], writes=[BSB["o"]])

    for b in range(BPC):
        for c in range(CH):
            for tb in range(NTB):
                T0 = tb * TBLK
                p.dma("sync", scr2[:, :], dr["oscr"][b, c * 128:(c + 1) * 128, T0:T0 + TBLK],
                      writes=[scr2])
                p.ts(wi32[:, 0:TBLK], scr2[:, :], scb[:, 3:4], ALU.mult,
                     reads=[scr2, scb], writes=[wi32])
                p.vcopy(xqT[:, c * T + T0:c * T + T0 + TBLK], wi32[:, 0:TBLK],
                        reads=[wi32], writes=[xqT])
        if debug and b == 0:
            p.dma("gpsimd", dr["d_yq"], xqT[:, :], reads=[xqT])
            p.dma("gpsimd", dr["d_scal2"], scal[:, :], reads=[scal])
        for co in range(CH):
            for tb in range(NTB):
                T0 = tb * TBLK
                for c in range(CH):
                    p.mm(ps_w2[:, :], WT["o"][:, c * D + co * 128:c * D + (co + 1) * 128],
                         xqT[:, c * T + T0:c * T + T0 + TBLK],
                         reads=[WT["o"], xqT], writes=[ps_w2],
                         start=(c == 0), stop=False)
                p.mm(ps_w2[:, :], BSC["o"][0:1, co * 128:(co + 1) * 128], onesbp[0:1, :],
                     reads=[BSB["o"], onesbp], writes=[ps_w2], start=False, stop=True)
                p.dma("sync", scr3[:, :], xb(b, co, T0, T0 + TBLK), writes=[scr3])
                p.stt(scr3[:, :], ps_w2[:, :], scb[:, 7:8], scr3[:, :],
                      ALU.mult, ALU.add, reads=[ps_w2, scb], writes=[scr3])
                p.dma("sync", xofm[b, co * 128:(co + 1) * 128, T0:T0 + TBLK], scr3[:, :],
                      reads=[scr3])



def build_l1a(debug=False):
    nc = bass.Bass("TRN2", target_bir_lowering=False, debug=False)
    dr = {}
    dr["x"] = nc.dram_tensor("x", [BPC, D, T], F32, kind="ExternalInput").ap()
    for w in ("wq", "wk", "wv", "wu"):
        dr[w] = nc.dram_tensor(w, [D, D], F32, kind="ExternalInput").ap()
    dr["bu"] = nc.dram_tensor("bu", [1, D], F32, kind="ExternalInput").ap()
    dr["xo"] = nc.dram_tensor("xo", [BPC, D, T], F32, kind="ExternalOutput").ap()
    dr["wscr"] = nc.dram_tensor("wscr", [D, D], BF16)
    if debug:
        for nm, shp in (("d_xn", [128, CH * T]), ("d_qT", [64, T]), ("d_kT", [64, T]),
                        ("d_vh", [128, NTT * 65]), ("d_negm", [1, NH]),
                        ("d_yn", [128, CH * T]), ("d_wtq", [128, CH * D])):
            dr[nm] = nc.dram_tensor(nm, shp, F32, kind="ExternalOutput").ap()
    with ExitStack() as stack:
        p = Prog(nc, stack)
        emit_l1a(nc, stack, p, dr, debug=debug)
        p.wait("sync", p.barrier_token_all())
        with nc.Block() as blk:
            p.finalize(blk)
    return nc


def emit_l1a(nc, stack, p, dr, debug=False):
    A = lambda n, s, d: SB(nc, stack, "c_" + n, s, d)
    WT = {w: A("WT" + w, [128, CH * D], BF16) for w in ("q", "k", "v", "u")}
    xnT = A("xnT", [128, CH * T], BF16)
    ynT = A("ynT", [128, CH * T], BF16)
    xblk = A("xblk", [128, CH * TBLK], F32)
    qT = A("qT", [64, T], BF16)
    kT = A("kT", [64, T], BF16)
    vh = A("vh", [128, NTT * 65], BF16)
    scr = A("scr", [128, TBLK], F32)
    scr2 = A("scr2", [128, TBLK], F32)
    scr3 = A("scr3", [128, TBLK], F32)
    sq64 = A("sq64", [64, TBLK], F32)
    pt = [A("pt0", [128, TBLK], BF16), A("pt1", [128, TBLK], BF16)]
    dnh = A("dnh", [1, TBLK], F32)
    stat = A("stat", [1, TBLK], F32)
    stat2 = A("stat2", [1, TBLK], F32)
    stat3 = A("stat3", [1, TBLK], F32)
    braw = A("braw", [1, D], F32)
    bsc = A("bsc", [1, D], BF16)
    scal = A("scal", [1, 32], F32)
    negm = A("negm", [1, NH], F32)
    negmb = A("negmb", [128, NH], F32)
    onesfp = A("onesfp", [65, 128], F32)
    onesfc = A("onesfc", [128, 1], F32)
    onesbp = A("onesbp", [1, TBLK], BF16)
    ones64 = A("ones64", [64, 1], F32)
    ps_s = [PS(nc, stack, "c_ps_s0", [128, TBLK]), PS(nc, stack, "c_ps_s1", [128, TBLK])]
    ps_o = [PS(nc, stack, "c_ps_o0", [65, TBLK]), PS(nc, stack, "c_ps_o1", [65, TBLK])]
    ps_w = PS(nc, stack, "c_ps_w", [128, TBLK])
    ps_w2 = PS(nc, stack, "c_ps_w2", [128, TBLK])
    ps_r = PS(nc, stack, "c_ps_r", [1, TBLK])

    p.memset(onesfp[:], 1.0, writes=[onesfp])
    p.memset(onesfc[:], 1.0, writes=[onesfc])
    p.memset(onesbp[:], 1.0, writes=[onesbp])
    p.memset(ones64[:], 1.0, writes=[ones64])
    p.dma("sync", braw[:], dr["bu"], writes=[braw])
    p.vcopy(bsc[:], braw[:], reads=[braw], writes=[bsc])

    xfm, xofm = dr["x"], dr["xo"]
    xb = lambda b, c, t0, t1: xfm[b, c * 128:(c + 1) * 128, t0:t1]
    QKSCALE = float(1.0 / (1024.0 ** 0.25))

    def xblk_load(b, T0):
        return p.dma("sync", xblk[:, :].rearrange("p (c t) -> p c t", c=CH),
                     xfm[b].rearrange("(c p) t -> p c t", p=128)[:, :, T0:T0 + TBLK],
                     writes=[xblk])

    for w in ("q", "k", "v", "u"):
        p.dma("gpsimd", WT[w][:, :].rearrange("p (c o) -> p c o", c=CH),
              dr["w" + w].rearrange("(c p) o -> p c o", p=128), writes=[WT[w]])
        p.dma("gpsimd", dr["wscr"].rearrange("(c p) o -> p c o", p=128),
              WT[w][:, :].rearrange("p (c o) -> p c o", c=CH), reads=[WT[w]],
              writes=[dr["wscr"]])
        for c in range(CH):
            p.dmat(WT[w][:, c * D:(c + 1) * D], dr["wscr"][:, c * 128:(c + 1) * 128],
                   reads=[dr["wscr"]],
                   writes=[WT[w]] if c == CH - 1 else [],
                   deps=[(p.sem["dg"], p.sem["dg"].n, None)])

    for b in range(BPC):
        for tb in range(NTB):
            T0 = tb * TBLK
            xblk_load(b, T0)
            for c in range(CH):
                p.mm(ps_r[:, :], onesfc[:, :], xblk[:, c * TBLK:(c + 1) * TBLK],
                     reads=[onesfc, xblk], writes=[ps_r],
                     start=(c == 0), stop=(c == CH - 1))
            p.ts(stat[:, :], ps_r[0:1, :], 1.0 / D, ALU.mult, reads=[ps_r], writes=[stat])
            for c in range(CH):
                p.act(scr[:, :], xblk[:, c * TBLK:(c + 1) * TBLK], AF.Square,
                      reads=[xblk], writes=[scr])
                p.mm(ps_r[:, :], onesfc[:, :], scr[:, :], reads=[onesfc, scr],
                     writes=[ps_r], start=(c == 0), stop=(c == CH - 1))
            p.ts(stat2[:, :], ps_r[0:1, :], 1.0 / D, ALU.mult, reads=[ps_r], writes=[stat2])
            p.act(stat3[:, :], stat[:, :], AF.Square, reads=[stat], writes=[stat3])
            p.tt(stat2[:, :], stat2[:, :], stat3[:, :], ALU.subtract,
                 reads=[stat3], writes=[stat2])
            p.ts(stat2[:, :], stat2[:, :], 1e-5, ALU.add, reads=[stat2], writes=[stat2])
            p.act(stat3[:, :], stat2[:, :], AF.Sqrt, reads=[stat2], writes=[stat3])
            p.vrecip(stat3[:, :], stat3[:, :], reads=[stat3], writes=[stat3])
            p.tt(stat[:, :], stat[:, :], stat3[:, :], ALU.mult, reads=[stat3], writes=[stat])
            p.ts(stat[:, :], stat[:, :], -1.0, ALU.mult, reads=[stat], writes=[stat])
            p.mm(ps_w[:, :], onesfp[0:1, :], stat3[0:1, :], reads=[onesfp, stat3], writes=[ps_w])
            p.act(scr[:, :], ps_w[:, :], reads=[ps_w], writes=[scr])
            p.mm(ps_w2[:, :], onesfp[0:1, :], stat[0:1, :], reads=[onesfp, stat], writes=[ps_w2])
            p.act(scr2[:, :], ps_w2[:, :], reads=[ps_w2], writes=[scr2])
            for c in range(CH):
                p.tt(scr3[:, :], xblk[:, c * TBLK:(c + 1) * TBLK], scr[:, :],
                     ALU.mult, reads=[xblk, scr], writes=[scr3])
                p.tt(xnT[:, c * T + T0:c * T + T0 + TBLK], scr3[:, :], scr2[:, :],
                     ALU.add, reads=[scr3, scr2], writes=[xnT])
        if debug and b == 0:
            p.dma("gpsimd", dr["d_xn"], xnT[:, :], reads=[xnT])
            p.dma("gpsimd", dr["d_wtq"], WT["q"][:, :], reads=[WT["q"]])
        for h in range(NH):
            o0 = h * HD
            for dst, w in ((qT, "q"), (kT, "k")):
                for tb in range(NTB):
                    T0 = tb * TBLK
                    for c in range(CH):
                        p.mm(ps_w[0:64, :], WT[w][:, c * D + o0:c * D + o0 + HD],
                             xnT[:, c * T + T0:c * T + T0 + TBLK],
                             reads=[WT[w], xnT], writes=[ps_w],
                             start=(c == 0), stop=(c == CH - 1))
                    p.act(dst[:, T0:T0 + TBLK], ps_w[0:64, :], AF.Identity,
                          scale=QKSCALE, reads=[ps_w], writes=[dst])
            for tt_ in range(NTT):
                t0 = tt_ * 128
                for c in range(CH):
                    p.mm(ps_w[:, 0:64], xnT[:, c * T + t0:c * T + t0 + 128],
                         WT["v"][:, c * D + o0:c * D + o0 + HD],
                         reads=[WT["v"], xnT], writes=[ps_w],
                         start=(c == 0), stop=(c == CH - 1))
                p.act(vh[:, tt_ * 65:tt_ * 65 + 64], ps_w[:, 0:64], AF.Identity,
                      reads=[ps_w], writes=[vh])
            if h == 0:
                p.memset(vh[:, :].rearrange("p (n k) -> p n k", k=65)[:, :, 64:65], 1.0,
                         writes=[vh])
            else:
                pass
            for qi, src_ in enumerate((qT, kT)):
                col = 26 + qi
                for tb in range(NTB):
                    T0 = tb * TBLK
                    p.act(sq64[:, :], src_[:, T0:T0 + TBLK], AF.Square,
                          reads=[src_], writes=[sq64])
                    p.mm(ps_r[:, :], ones64[:, :], sq64[:, :],
                         reads=[ones64, sq64], writes=[ps_r])
                    p.vred(stat[:, 0:1], ps_r[0:1, :], ALU.max, False,
                           reads=[ps_r], writes=[stat])
                    if tb == 0:
                        p.vcopy(scal[:, col:col + 1], stat[:, 0:1], reads=[stat], writes=[scal])
                    else:
                        p.tt(scal[:, col:col + 1], scal[:, col:col + 1], stat[:, 0:1],
                             ALU.max, reads=[stat], writes=[scal])
            p.tt(scal[:, 28:29], scal[:, 26:27], scal[:, 27:28], ALU.mult,
                 reads=[scal], writes=[scal])
            p.act(scal[:, 29:30], scal[:, 28:29], AF.Sqrt, reads=[scal], writes=[scal])
            p.ts(negm[:, h:h + 1], scal[:, 29:30], -1.0, ALU.mult,
                 reads=[scal], writes=[negm])
            p.mm(ps_w2[:, 8:9], onesfp[0:1, :], negm[0:1, h:h + 1],
                 reads=[onesfp, negm], writes=[ps_w2])
            p.act(negmb[:, h:h + 1], ps_w2[:, 8:9], reads=[ps_w2], writes=[negmb])
            if debug and b == 0 and h == 0:
                p.dma("gpsimd", dr["d_qT"], qT[:, :], reads=[qT])
                p.dma("gpsimd", dr["d_kT"], kT[:, :], reads=[kT])
                p.dma("gpsimd", dr["d_vh"], vh[:, :], reads=[vh])
                p.dma("gpsimd", dr["d_negm"], negm[:, :], reads=[negm])
            for j0 in range(NTB):
                T0 = j0 * TBLK
                po = ps_o[j0 % 2]
                for a in range(NTT):
                    s_ = ps_s[a % 2]
                    pe = pt[a % 2]
                    p.mm(s_[:, :], kT[:, a * 128:(a + 1) * 128], qT[:, T0:T0 + TBLK],
                         reads=[kT, qT], writes=[s_])
                    p.act(pe[:, :], s_[:, :], AF.Exp, bias=negmb[:, h:h + 1],
                          reads=[s_, negmb], writes=[pe])
                    p.mm(po[:, :], vh[:, a * 65:a * 65 + 65], pe[:, :],
                         reads=[vh, pe], writes=[po],
                         start=(a == 0), stop=(a == NTT - 1))
                p.vrecip(dnh[:, :], po[64:65, :], reads=[po], writes=[dnh])
                p.mm(ps_w[0:64, :], onesfp[0:1, 0:64], dnh[0:1, :],
                     reads=[onesfp, dnh], writes=[ps_w])
                p.act(scr[0:64, :], ps_w[0:64, :], reads=[ps_w], writes=[scr])
                p.tt(ynT[(h % 2) * 64:(h % 2) * 64 + 64,
                         (h // 2) * T + T0:(h // 2) * T + T0 + TBLK],
                     po[0:64, :], scr[0:64, :], ALU.mult,
                     reads=[po, scr], writes=[ynT])
        if debug and b == 0:
            p.dma("gpsimd", dr["d_yn"], ynT[:, :], reads=[ynT])
        for co in range(CH):
            for tb in range(NTB):
                T0 = tb * TBLK
                for c in range(CH):
                    p.mm(ps_w2[:, :], WT["u"][:, c * D + co * 128:c * D + (co + 1) * 128],
                         ynT[:, c * T + T0:c * T + T0 + TBLK],
                         reads=[WT["u"], ynT], writes=[ps_w2],
                         start=(c == 0), stop=False)
                p.mm(ps_w2[:, :], bsc[0:1, co * 128:(co + 1) * 128], onesbp[0:1, :],
                     reads=[bsc, onesbp], writes=[ps_w2], start=False, stop=True)
                p.dma("sync", scr3[:, :], xb(b, co, T0, T0 + TBLK), writes=[scr3])
                p.tt(scr3[:, :], ps_w2[:, :], scr3[:, :], ALU.add,
                     reads=[ps_w2], writes=[scr3])
                p.dma("sync", xofm[b, co * 128:(co + 1) * 128, T0:T0 + TBLK], scr3[:, :],
                      reads=[scr3])


TBF = 128


def build_l1f(debug=False):
    nc = bass.Bass("TRN2", target_bir_lowering=False, debug=False)
    dr = {}
    dr["x"] = nc.dram_tensor("x", [BPC, D, T], F32, kind="ExternalInput").ap()
    dr["w1"] = nc.dram_tensor("w1", [4096, D], F32, kind="ExternalInput").ap()
    dr["w2"] = nc.dram_tensor("w2", [D, 4096], F32, kind="ExternalInput").ap()
    dr["cw"] = nc.dram_tensor("cw", [4096, 3], F32, kind="ExternalInput").ap()
    dr["cb"] = nc.dram_tensor("cb", [128, 32], F32, kind="ExternalInput").ap()
    dr["xo"] = nc.dram_tensor("xo", [BPC, D, T], F32, kind="ExternalOutput").ap()
    dr["w1scr"] = nc.dram_tensor("w1scr", [4096, D], BF16)
    dr["w2scr"] = nc.dram_tensor("w2scr", [D, 4096], BF16)
    with ExitStack() as stack:
        p = Prog(nc, stack)
        emit_l1f(nc, stack, p, dr, debug=debug)
        p.wait("sync", p.barrier_token_all())
        with nc.Block() as blk:
            p.finalize(blk)
    return nc


def emit_l1f(nc, stack, p, dr, debug=False):
    A = lambda n, s, d: SB(nc, stack, "f_" + n, s, d)
    NH1 = 32
    W1T = A("W1T", [128, CH * 4096], BF16)
    W2T = A("W2T", [128, NH1 * D], BF16)
    wst = A("wst", [128, 4096], F32)
    hbl = A("hbl", [128, NH1 * (TBF + 2)], F32)
    hs = A("hs", [128, NH1 * TBF], F32)
    hq = A("hq", [128, NH1 * TBF], BF16)
    xblk = A("xblk", [128, CH * (TBF + 2)], F32)
    xq = A("xq", [128, CH * (TBF + 2)], BF16)
    wi32 = A("wi32", [128, TBF + 2], I32)
    scr = A("scr", [128, TBLK], F32)
    scr2 = A("scr2", [128, TBLK], F32)
    scr3 = A("scr3", [128, TBLK], F32)
    zrow = A("zrow", [1, TBF + 2], F32)
    zrow2 = A("zrow2", [1, TBF + 2], F32)
    stat = A("stat", [1, TBLK], F32)
    stat2 = A("stat2", [1, TBLK], F32)
    stat3 = A("stat3", [1, TBLK], F32)
    cwp = A("cwp", [128, 3 * NH1], F32)
    cbp = A("cbp", [128, NH1], F32)
    w1x = A("w1x", [128, NH1], F32)
    r0p = A("r0p", [128, NH1], F32)
    r2p = A("r2p", [128, NH1], F32)
    asum = A("asum", [128, 1], F32)
    rtmp = A("rtmp", [128, 1], F32)
    scal = A("scal", [1, 32], F32)
    onesfp = A("onesfp", [65, 128], F32)
    onesfc = A("onesfc", [128, 1], F32)
    ps_a = PS(nc, stack, "f_ps_a", [128, TBLK])
    ps_b = PS(nc, stack, "f_ps_b", [128, TBLK])
    ps_w = PS(nc, stack, "f_ps_w", [128, TBLK])
    ps_w2 = PS(nc, stack, "f_ps_w2", [128, TBLK])
    ps_r = PS(nc, stack, "f_ps_r", [1, TBLK])

    p.memset(onesfp[:], 1.0, writes=[onesfp])
    p.memset(onesfc[:], 1.0, writes=[onesfc])
    xfm, xofm = dr["x"], dr["xo"]

    p.dma("sync", cwp[:, :].rearrange("p (cc k) -> p cc k", k=3),
          dr["cw"].rearrange("(cc p) k -> p cc k", p=128), writes=[cwp])
    p.dma("sync", cbp[:, :], dr["cb"], writes=[cbp])
    cw1 = cwp[:, :].rearrange("p (cc k) -> p k cc", k=3)
    for cc in range(NH1):
        p.vcopy(w1x[:, cc:cc + 1], cwp[:, 3 * cc + 1:3 * cc + 2], reads=[cwp], writes=[w1x])
    p.do("scalar", lambda h: h.activation(out=scr[:, 0:NH1], in_=w1x[:, :], func=AF.Abs),
         reads=[w1x], writes=[scr])
    p.ts(scr[:, 32:32 + NH1], scr[:, 0:NH1], 1e-30, ALU.is_lt, reads=[scr], writes=[scr])
    p.ts(scr[:, 64:64 + NH1], scr[:, 32:32 + NH1], 1e-30, ALU.mult, reads=[scr], writes=[scr])
    p.tt(w1x[:, :], w1x[:, :], scr[:, 64:64 + NH1], ALU.add, reads=[scr], writes=[w1x])
    p.vrecip(scr[:, 0:NH1], w1x[:, :], reads=[w1x], writes=[scr])
    for cc in range(NH1):
        p.tt(r0p[:, cc:cc + 1], cwp[:, 3 * cc:3 * cc + 1], scr[:, cc:cc + 1],
             ALU.mult, reads=[cwp, scr], writes=[r0p])
        p.tt(r2p[:, cc:cc + 1], cwp[:, 3 * cc + 2:3 * cc + 3], scr[:, cc:cc + 1],
             ALU.mult, reads=[cwp, scr], writes=[r2p])

    def tern_weights(src_ap, scr_dram, nrow_chunks, scol):
        ncols = src_ap.shape[1]
        nsb = (ncols + 511) // 512
        p.memset(asum[:], 0.0, writes=[asum])
        for rc in range(nrow_chunks):
            p.dma("sync", wst[:, 0:ncols], src_ap[rc * 128:(rc + 1) * 128, :], writes=[wst])
            for sb in range(nsb):
                c0, c1 = sb * 512, min((sb + 1) * 512, ncols)
                p.do("scalar",
                     lambda h, a=c0, bnd=c1: h.activation(out=scr[:, 0:bnd - a],
                                                          in_=wst[:, a:bnd], func=AF.Abs),
                     reads=[wst], writes=[scr])
                p.vred(rtmp[:], scr[:, 0:c1 - c0], ALU.add, False, reads=[scr], writes=[rtmp])
                p.tt(asum[:], asum[:], rtmp[:], ALU.add, reads=[rtmp], writes=[asum])
        p.mm(ps_r[:, 0:1], onesfc[:, :], asum[:, :], reads=[onesfc, asum], writes=[ps_r])
        p.ts(stat[:, 0:1], ps_r[0:1, 0:1], 1.0 / (nrow_chunks * 128 * ncols), ALU.mult,
             reads=[ps_r], writes=[stat])
        p.ts(stat[:, 1:2], stat[:, 0:1], 1e-5, ALU.max, reads=[stat], writes=[stat])
        p.vcopy(scal[:, scol:scol + 1], stat[:, 1:2], reads=[stat], writes=[scal])
        p.vrecip(stat[:, 2:3], stat[:, 1:2], reads=[stat], writes=[stat])
        p.mm(ps_w[:, 0:1], onesfp[0:1, :], stat[0:1, 2:3], reads=[onesfp, stat], writes=[ps_w])
        p.act(scr3[:, 0:1], ps_w[:, 0:1], reads=[ps_w], writes=[scr3])
        for rc in range(nrow_chunks):
            p.dma("sync", wst[:, 0:ncols], src_ap[rc * 128:(rc + 1) * 128, :], writes=[wst])
            p.ts(wst[:, 0:ncols], wst[:, 0:ncols], scr3[:, 0:1], ALU.mult,
                 reads=[scr3], writes=[wst])
            for sb in range(nsb):
                c0, c1 = sb * 512, min((sb + 1) * 512, ncols)
                p.ts(scr[:, 0:c1 - c0], wst[:, c0:c1], 0.5, ALU.is_gt,
                     reads=[wst], writes=[scr])
                p.ts(scr2[:, 0:c1 - c0], wst[:, c0:c1], -0.5, ALU.is_lt,
                     reads=[wst], writes=[scr2])
                p.tt(scr[:, 0:c1 - c0], scr[:, 0:c1 - c0], scr2[:, 0:c1 - c0],
                     ALU.subtract, reads=[scr2], writes=[scr])
                p.dma("gpsimd", scr_dram[rc * 128:(rc + 1) * 128, c0:c1],
                      scr[:, 0:c1 - c0], reads=[scr])

    tern_weights(dr["w1"], dr["w1scr"].ap(), 32, 20)
    tern_weights(dr["w2"], dr["w2scr"].ap(), 8, 21)

    for c in range(CH):
        p.dmat(W1T[:, c * 4096:(c + 1) * 4096], dr["w1scr"][:, c * 128:(c + 1) * 128],
               writes=[W1T] if c == CH - 1 else [],
               deps=[(p.sem["dg"], p.sem["dg"].n, None)])
    for hc in range(NH1):
        p.dmat(W2T[:, hc * D:(hc + 1) * D], dr["w2scr"][:, hc * 128:(hc + 1) * 128],
               writes=[W2T] if hc == NH1 - 1 else [],
               deps=[(p.sem["dg"], p.sem["dg"].n, None)])

    NBF = T // TBF
    for b in range(BPC):
        for blk_i in range(NBF):
            T0 = blk_i * TBF
            lo = 1 if blk_i == 0 else 0
            hi = 1 if blk_i == NBF - 1 else 0
            t_lo = max(T0 - 1, 0)
            t_hi = min(T0 + TBF + 1, T)
            ncol = t_hi - t_lo
            dst_off = 1 - (T0 - t_lo)
            if lo:
                p.memset(xblk[:, :].rearrange("p (c t) -> p c t", c=CH)[:, :, 0:1], 0.0,
                         writes=[xblk])
            if hi:
                p.memset(xblk[:, :].rearrange("p (c t) -> p c t", c=CH)[:, :, TBF + 1:TBF + 2],
                         0.0, writes=[xblk])
            p.dma("sync",
                  xblk[:, :].rearrange("p (c t) -> p c t", c=CH)[:, :, dst_off:dst_off + ncol],
                  xfm[b].rearrange("(c p) t -> p c t", p=128)[:, :, t_lo:t_hi],
                  writes=[xblk])
            W = TBF + 2
            for c in range(CH):
                p.mm(ps_r[:, 0:W], onesfc[:, :], xblk[:, c * W:c * W + W],
                     reads=[onesfc, xblk], writes=[ps_r],
                     start=(c == 0), stop=(c == CH - 1))
            p.ts(zrow[:, :], ps_r[0:1, 0:W], 1.0 / D, ALU.mult, reads=[ps_r], writes=[zrow])
            for c in range(CH):
                p.act(scr[:, 0:W], xblk[:, c * W:c * W + W], AF.Square,
                      reads=[xblk], writes=[scr])
                p.mm(ps_r[:, 0:W], onesfc[:, :], scr[:, 0:W], reads=[onesfc, scr],
                     writes=[ps_r], start=(c == 0), stop=(c == CH - 1))
            p.ts(zrow2[:, :], ps_r[0:1, 0:W], 1.0 / D, ALU.mult, reads=[ps_r], writes=[zrow2])
            p.act(stat[:, 0:W], zrow[0:1, :], AF.Square, reads=[zrow], writes=[stat])
            p.tt(zrow2[:, :], zrow2[:, :], stat[0:1, 0:W], ALU.subtract,
                 reads=[stat], writes=[zrow2])
            p.ts(zrow2[:, :], zrow2[:, :], 1e-5, ALU.add, reads=[zrow2], writes=[zrow2])
            p.act(zrow2[:, :], zrow2[:, :], AF.Sqrt, reads=[zrow2], writes=[zrow2])
            p.vrecip(zrow2[:, :], zrow2[:, :], reads=[zrow2], writes=[zrow2])
            p.tt(zrow[:, :], zrow[:, :], zrow2[:, :], ALU.mult, reads=[zrow2], writes=[zrow])
            p.ts(zrow[:, :], zrow[:, :], -1.0, ALU.mult, reads=[zrow], writes=[zrow])
            p.mm(ps_w[:, 0:W], onesfp[0:1, :], zrow2[0:1, :], reads=[onesfp, zrow2], writes=[ps_w])
            p.act(scr[:, 0:W], ps_w[:, 0:W], reads=[ps_w], writes=[scr])
            p.mm(ps_w2[:, 0:W], onesfp[0:1, :], zrow[0:1, :], reads=[onesfp, zrow], writes=[ps_w2])
            p.act(scr2[:, 0:W], ps_w2[:, 0:W], reads=[ps_w2], writes=[scr2])
            for c in range(CH):
                p.tt(scr3[:, 0:W], xblk[:, c * W:c * W + W], scr[:, 0:W], ALU.mult,
                     reads=[xblk, scr], writes=[scr3])
                p.tt(xblk[:, c * W:c * W + W], scr3[:, 0:W], scr2[:, 0:W], ALU.add,
                     reads=[scr3, scr2], writes=[xblk])
            p.tt(scr[:, 0:W], xblk[:, 0 * W:0 * W + W], xblk[:, 1 * W:1 * W + W],
                 ALU.max, reads=[xblk], writes=[scr])
            p.tt(scr3[:, 0:W], xblk[:, 0 * W:0 * W + W], xblk[:, 1 * W:1 * W + W],
                 ALU.min, reads=[xblk], writes=[scr3])
            for c in range(2, CH):
                p.tt(scr[:, 0:W], scr[:, 0:W], xblk[:, c * W:c * W + W],
                     ALU.max, reads=[xblk], writes=[scr])
                p.tt(scr3[:, 0:W], scr3[:, 0:W], xblk[:, c * W:c * W + W],
                     ALU.min, reads=[xblk], writes=[scr3])
            p.ts(scr3[:, 0:W], scr3[:, 0:W], -1.0, ALU.mult, reads=[scr3], writes=[scr3])
            p.tt(scr[:, 0:W], scr[:, 0:W], scr3[:, 0:W], ALU.max,
                 reads=[scr3], writes=[scr])
            p.do("gpsimd", lambda h: h.tensor_reduce(out=zrow[0:1, 0:W], in_=scr[:, 0:W],
                                                     axis=AX.C, op=ALU.max,
                                                     apply_absolute_value=False),
                 reads=[scr], writes=[zrow])
            p.ts(zrow[:, :], zrow[:, :], 1e-5, ALU.max, reads=[zrow], writes=[zrow])
            p.vrecip(zrow2[:, :], zrow[0:1, :], reads=[zrow], writes=[zrow2])
            p.ts(zrow2[:, :], zrow2[:, :], 127.0, ALU.mult, reads=[zrow2], writes=[zrow2])
            p.mm(ps_w[:, 0:W], onesfp[0:1, :], zrow2[0:1, :], reads=[onesfp, zrow2], writes=[ps_w])
            p.act(scr[:, 0:W], ps_w[:, 0:W], reads=[ps_w], writes=[scr])
            for c in range(CH):
                p.tt(wi32[:, 0:W], xblk[:, c * W:c * W + W], scr[:, 0:W], ALU.mult,
                     reads=[xblk, scr], writes=[wi32])
                p.vcopy(xq[:, c * W:c * W + W], wi32[:, 0:W], reads=[wi32], writes=[xq])
            p.ts(stat[:, 0:W], zrow[0:1, :], scal[:, 20:21], ALU.mult, reads=[zrow, scal],
                 writes=[stat])
            p.ts(stat[:, 0:W], stat[:, 0:W], 1.0 / 127.0, ALU.mult, reads=[stat], writes=[stat])
            p.mm(ps_w[:, 0:W], onesfp[0:1, :], stat[0:1, 0:W], reads=[onesfp, stat], writes=[ps_w])
            p.act(scr2[:, 0:W], ps_w[:, 0:W], reads=[ps_w], writes=[scr2])
            for mh in range(NH1):
                pscur = ps_a if mh % 2 == 0 else ps_b
                for c in range(CH):
                    p.mm(pscur[:, 0:W], W1T[:, c * 4096 + mh * 128:c * 4096 + (mh + 1) * 128],
                         xq[:, c * W:c * W + W], reads=[W1T, xq], writes=[pscur],
                         start=(c == 0), stop=(c == CH - 1))
                p.tt(hbl[:, mh * W:mh * W + W], pscur[:, 0:W], scr2[:, 0:W], ALU.mult,
                     reads=[pscur, scr2], writes=[hbl])
                p.stt(scr3[:, 0:TBF], hbl[:, mh * W:mh * W + TBF], r0p[:, mh:mh + 1],
                      hbl[:, mh * W + 1:mh * W + 1 + TBF], ALU.mult, ALU.add,
                      reads=[hbl, r0p], writes=[scr3])
                p.stt(scr3[:, 0:TBF], hbl[:, mh * W + 2:mh * W + 2 + TBF], r2p[:, mh:mh + 1],
                      scr3[:, 0:TBF], ALU.mult, ALU.add, reads=[hbl, r2p], writes=[scr3])
                p.act(hs[:, mh * TBF:(mh + 1) * TBF], scr3[:, 0:TBF], AF.Silu,
                      scale=w1x[:, mh:mh + 1], bias=cbp[:, mh:mh + 1],
                      reads=[scr3, w1x, cbp], writes=[hs])
            p.tt(scr[:, 0:TBF], hs[:, 0:TBF], hs[:, TBF:2 * TBF], ALU.max,
                 reads=[hs], writes=[scr])
            p.tt(scr3[:, 0:TBF], hs[:, 0:TBF], hs[:, TBF:2 * TBF], ALU.min,
                 reads=[hs], writes=[scr3])
            for mh in range(2, NH1):
                p.tt(scr[:, 0:TBF], scr[:, 0:TBF], hs[:, mh * TBF:(mh + 1) * TBF],
                     ALU.max, reads=[hs], writes=[scr])
                p.tt(scr3[:, 0:TBF], scr3[:, 0:TBF], hs[:, mh * TBF:(mh + 1) * TBF],
                     ALU.min, reads=[hs], writes=[scr3])
            p.ts(scr3[:, 0:TBF], scr3[:, 0:TBF], -1.0, ALU.mult, reads=[scr3], writes=[scr3])
            p.tt(scr[:, 0:TBF], scr[:, 0:TBF], scr3[:, 0:TBF], ALU.max,
                 reads=[scr3], writes=[scr])
            p.do("gpsimd", lambda h: h.tensor_reduce(out=zrow[0:1, 0:TBF], in_=scr[:, 0:TBF],
                                                     axis=AX.C, op=ALU.max,
                                                     apply_absolute_value=False),
                 reads=[scr], writes=[zrow])
            p.ts(zrow[:, 0:TBF], zrow[:, 0:TBF], 1e-5, ALU.max, reads=[zrow], writes=[zrow])
            p.vrecip(zrow2[:, 0:TBF], zrow[0:1, 0:TBF], reads=[zrow], writes=[zrow2])
            p.ts(zrow2[:, 0:TBF], zrow2[:, 0:TBF], 127.0, ALU.mult, reads=[zrow2], writes=[zrow2])
            p.mm(ps_w[:, 0:TBF], onesfp[0:1, :], zrow2[0:1, 0:TBF],
                 reads=[onesfp, zrow2], writes=[ps_w])
            p.act(scr[:, 0:TBF], ps_w[:, 0:TBF], reads=[ps_w], writes=[scr])
            for mh in range(NH1):
                p.tt(wi32[:, 0:TBF], hs[:, mh * TBF:(mh + 1) * TBF], scr[:, 0:TBF],
                     ALU.mult, reads=[hs, scr], writes=[wi32])
                p.vcopy(hq[:, mh * TBF:(mh + 1) * TBF], wi32[:, 0:TBF],
                        reads=[wi32], writes=[hq])
            p.ts(stat[:, 0:TBF], zrow[0:1, 0:TBF], scal[:, 21:22], ALU.mult,
                 reads=[zrow, scal], writes=[stat])
            p.ts(stat[:, 0:TBF], stat[:, 0:TBF], 1.0 / 127.0, ALU.mult, reads=[stat], writes=[stat])
            p.mm(ps_w[:, 0:TBF], onesfp[0:1, :], stat[0:1, 0:TBF],
                 reads=[onesfp, stat], writes=[ps_w])
            p.act(scr2[:, 0:TBF], ps_w[:, 0:TBF], reads=[ps_w], writes=[scr2])
            for do in range(CH):
                pscur = ps_a if do % 2 == 0 else ps_b
                for hc in range(NH1):
                    p.mm(pscur[:, 0:TBF], W2T[:, hc * D + do * 128:hc * D + (do + 1) * 128],
                         hq[:, hc * TBF:(hc + 1) * TBF], reads=[W2T, hq], writes=[pscur],
                         start=(hc == 0), stop=(hc == NH1 - 1))
                p.tt(scr3[:, 0:TBF], pscur[:, 0:TBF], scr2[:, 0:TBF], ALU.mult,
                     reads=[pscur, scr2], writes=[scr3])
                p.dma("sync", scr[:, 0:TBF], xfm[b, do * 128:(do + 1) * 128, T0:T0 + TBF],
                      writes=[scr])
                p.tt(scr3[:, 0:TBF], scr3[:, 0:TBF], scr[:, 0:TBF], ALU.add,
                     reads=[scr], writes=[scr3])
                p.dma("sync", xofm[b, do * 128:(do + 1) * 128, T0:T0 + TBF], scr3[:, 0:TBF],
                      reads=[scr3])



def build_l0f(debug=False):
    nc = bass.Bass("TRN2", target_bir_lowering=False, debug=False)
    dr = {}
    dr["x"] = nc.dram_tensor("x", [BPC, D, T], F32, kind="ExternalInput").ap()
    for w in ("wq", "wk", "wv", "wo"):
        dr[w] = nc.dram_tensor(w, [D, D], F32, kind="ExternalInput").ap()
    dr["scale1"] = nc.dram_tensor("scale1", [128, CH], F32, kind="ExternalInput").ap()
    dr["scale2"] = nc.dram_tensor("scale2", [128, CH], F32, kind="ExternalInput").ap()
    dr["rw"] = nc.dram_tensor("rw", [D, 8], F32, kind="ExternalInput").ap()
    dr["w1"] = nc.dram_tensor("w1", [8, D, D], F32, kind="ExternalInput").ap()
    dr["w2"] = nc.dram_tensor("w2", [8, D, D], F32, kind="ExternalInput").ap()
    dr["wout"] = nc.dram_tensor("wout", [8, D, D], F32, kind="ExternalInput").ap()
    dr["b1"] = nc.dram_tensor("b1", [8, D], F32, kind="ExternalInput").ap()
    dr["b2"] = nc.dram_tensor("b2", [8, D], F32, kind="ExternalInput").ap()
    dr["bout"] = nc.dram_tensor("bout", [8, D], F32, kind="ExternalInput").ap()
    dr["gum"] = nc.dram_tensor("gum", [BPC, T, 8], F32, kind="ExternalInput").ap()
    dr["cmask"] = nc.dram_tensor("cmask", [128, 4 * 512], F32, kind="ExternalInput").ap()
    dr["rsin"] = nc.dram_tensor("rsin", [32, T], F32, kind="ExternalInput").ap()
    dr["rcos"] = nc.dram_tensor("rcos", [32, T], F32, kind="ExternalInput").ap()
    dr["e8"] = nc.dram_tensor("e8", [8, D], F32, kind="ExternalInput").ap()
    dr["ident"] = nc.dram_tensor("ident", [128, 128], F32, kind="ExternalInput").ap()
    dr["xo"] = nc.dram_tensor("xo", [BPC, D, T], F32, kind="ExternalOutput").ap()
    dr["hscr"] = nc.dram_tensor("hscr", [BPC, D, T], F32,
                                kind="ExternalOutput" if debug else "Internal")
    dr["uscr"] = nc.dram_tensor("uscr", [BPC, D, T], F32,
                                kind="ExternalOutput" if debug else "Internal")
    dr["rrow"] = nc.dram_tensor("rrow", [BPC, 1, T], F32)
    dr["eoscr"] = nc.dram_tensor("eoscr", [D, T], F32)
    if debug:
        for nm, shp in (("d_xr", [128, CH * T]), ("d_qT", [64, T]), ("d_kT", [64, T]),
                        ("d_vh", [128, NTT * 65]), ("d_negm", [1, NH]),
                        ("d_yn", [128, CH * T]), ("d_wgT", [8, T]),
                        ("d_h1", [128, CH * TBLK]), ("d_scal", [1, 32])):
            dr[nm] = nc.dram_tensor(nm, shp, F32, kind="ExternalOutput").ap()
    with ExitStack() as stack:
        p = Prog(nc, stack)
        emit_l0f(nc, stack, p, dr, debug=debug)
        p.wait("sync", p.barrier_token_all())
        with nc.Block() as blk:
            p.finalize(blk)
    return nc


def emit_l0f(nc, stack, p, dr, debug=False):
    A = lambda n, s, d: SB(nc, stack, "g_" + n, s, d)
    WB = {w: A("WB" + w, [128, CH * D], BF16) for w in ("q", "k", "v")}
    WBo = A("WBo", [128, CH * D], BF16)
    xrT = A("xrT", [128, CH * T], BF16)
    ynT = A("ynT", [128, CH * T], BF16)
    xblk = A("xblk", [128, CH * TBLK], F32)
    qT = A("qT", [64, T], BF16)
    kT = A("kT", [64, T], BF16)
    vh = A("vh", [128, NTT * 65], BF16)
    rpS = A("rpS", [64, T], BF16)
    rpC = A("rpC", [64, T], BF16)
    rn1 = A("rn1", [32, T], BF16)
    rn2 = A("rn2", [32, T], BF16)
    rs1 = A("rs1", [32, T], BF16)
    rs2 = A("rs2", [32, T], BF16)
    scr = A("scr", [128, TBLK], F32)
    scr2 = A("scr2", [128, TBLK], F32)
    scr3 = A("scr3", [128, TBLK], F32)
    pt = [A("pt0", [128, TBLK], BF16), A("pt1", [128, TBLK], BF16)]
    cmask = A("cmask", [128, 4 * 512], BF16)
    dnh = A("dnh", [1, TBLK], F32)
    stat = A("stat", [1, TBLK], F32)
    stat2 = A("stat2", [1, TBLK], F32)
    stat3 = dnh
    s1sb = A("s1sb", [128, CH], F32)
    s2sb = A("s2sb", [128, CH], F32)
    rwsb = A("rwsb", [128, CH * 8], F32)
    rw2sb = A("rw2sb", [128, CH * 8], F32)
    rcol = A("rcol", [128, 1], F32)
    e8sb = A("e8sb", [8, D], BF16)
    identb = A("identb", [128, 128], BF16)
    rw2b = A("rw2b", [128, CH * 8], BF16)
    prbb = A("prbb", [128, 8], BF16)
    wgT = A("wgT", [8, T], BF16)
    boutsb2 = A("boutsb2", [8, D], BF16)
    gum = A("gum", [128, 8], F32)
    lgt = A("lgt", [128, 8], F32)
    prb = A("prb", [128, 8], F32)
    colA = A("colA", [128, 1], F32)
    colB = A("colB", [128, 1], F32)
    scal = A("scal", [1, 32], F32)
    negm = A("negm", [1, NH], F32)
    negmb = A("negmb", [128, NH], F32)
    onesfp = A("onesfp", [65, 128], F32)
    onesfc = A("onesfc", [128, 1], F32)
    onesbp = A("onesbp", [65, TBLK], BF16)
    ones64 = A("ones64", [64, 1], F32)
    h1s = ynT
    b1b = vh[0:1, 0:D]
    b2b = vh[32:33, 0:D]
    rtb = vh[:, 0:D]
    boutsb = boutsb2
    ps_s = [PS(nc, stack, "g_ps_s0", [128, TBLK]), PS(nc, stack, "g_ps_s1", [128, TBLK])]
    ps_o = [PS(nc, stack, "g_ps_o0", [65, TBLK]), PS(nc, stack, "g_ps_o1", [65, TBLK])]
    ps_w = PS(nc, stack, "g_ps_w", [128, TBLK])
    ps_w2 = PS(nc, stack, "g_ps_w2", [128, TBLK])
    ps_r = PS(nc, stack, "g_ps_r", [1, TBLK])

    p.memset(onesfp[:], 1.0, writes=[onesfp])
    p.memset(onesfc[:], 1.0, writes=[onesfc])
    p.memset(onesbp[:], 1.0, writes=[onesbp])
    p.memset(ones64[:], 1.0, writes=[ones64])
    p.dma("gpsimd", cmask[:], dr["cmask"], writes=[cmask])
    p.dma("gpsimd", rpS[0:32, :], dr["rsin"], writes=[rpS])
    p.dma("gpsimd", rpS[32:64, :], dr["rsin"], writes=[rpS])
    p.dma("gpsimd", rpC[0:32, :], dr["rcos"], writes=[rpC])
    p.dma("gpsimd", rpC[32:64, :], dr["rcos"], writes=[rpC])
    p.dma("sync", s1sb[:], dr["scale1"], writes=[s1sb])
    p.dma("sync", s2sb[:], dr["scale2"], writes=[s2sb])
    p.dma("sync", rwsb[:, :].rearrange("p (c e) -> p c e", c=CH),
          dr["rw"].rearrange("(c p) e -> p c e", p=128), writes=[rwsb])

    p.dma("gpsimd", e8sb[:], dr["e8"], writes=[e8sb])
    p.dma("gpsimd", identb[:], dr["ident"], writes=[identb])
    for c_ in range(CH):
        p.ts(rw2sb[:, c_ * 8:(c_ + 1) * 8], rwsb[:, c_ * 8:(c_ + 1) * 8],
             s2sb[:, c_:c_ + 1], ALU.mult, reads=[rwsb, s2sb], writes=[rw2sb])
    p.vcopy(rw2b[:, :], rw2sb[:, :], reads=[rw2sb], writes=[rw2b])
    p.dma("gpsimd", WBo[:, :].rearrange("p (c o) -> p c o", c=CH),
          dr["wo"].rearrange("(c p) o -> p c o", p=128), writes=[WBo])

    xfm, xofm = dr["x"], dr["xo"]

    def rmsT(srcdram, b, ssb, dst, eps, rrow_out=None):
        for tb in range(NTB):
            T0 = tb * TBLK
            p.dma("sync", xblk[:, :].rearrange("p (c t) -> p c t", c=CH),
                  srcdram[b].rearrange("(c p) t -> p c t", p=128)[:, :, T0:T0 + TBLK],
                  writes=[xblk])
            for c in range(CH):
                p.act(scr[:, :], xblk[:, c * TBLK:(c + 1) * TBLK], AF.Square,
                      reads=[xblk], writes=[scr])
                p.mm(ps_r[:, :], onesfc[:, :], scr[:, :], reads=[onesfc, scr],
                     writes=[ps_r], start=(c == 0), stop=(c == CH - 1))
            p.ts(stat[:, :], ps_r[0:1, :], 1.0 / D, ALU.mult, reads=[ps_r], writes=[stat])
            p.ts(stat[:, :], stat[:, :], eps, ALU.add, reads=[stat], writes=[stat])
            p.act(stat2[:, :], stat[:, :], AF.Sqrt, reads=[stat], writes=[stat2])
            p.vrecip(stat3[:, :], stat2[:, :], reads=[stat2], writes=[stat3])
            if rrow_out is not None:
                p.dma("sync", rrow_out[b, 0:1, T0:T0 + TBLK], stat3[0:1, :],
                      reads=[stat3], writes=[rrow_out])
            p.mm(ps_w[:, :], onesfp[0:1, :], stat3[0:1, :], reads=[onesfp, stat3],
                 writes=[ps_w])
            p.act(scr[:, :], ps_w[:, :], reads=[ps_w], writes=[scr])
            for c in range(CH):
                p.tt(scr3[:, :], xblk[:, c * TBLK:(c + 1) * TBLK], scr[:, :],
                     ALU.mult, reads=[xblk, scr], writes=[scr3])
                p.ts(dst[:, c * T + T0:c * T + T0 + TBLK], scr3[:, :], ssb[:, c:c + 1],
                     ALU.mult, reads=[scr3, s1sb, s2sb], writes=[dst])

    def rope(dst):
        p.tt(rs1[:, :], dst[0:32, :], rpC[0:32, :], ALU.mult, reads=[dst, rpC], writes=[rs1])
        p.tt(rs2[:, :], dst[32:64, :], rpS[32:64, :], ALU.mult, reads=[dst, rpS], writes=[rs2])
        p.tt(rn1[:, :], rs1[:, :], rs2[:, :], ALU.subtract, reads=[rs1, rs2], writes=[rn1])
        p.tt(rs1[:, :], dst[0:32, :], rpS[0:32, :], ALU.mult, reads=[dst, rpS], writes=[rs1])
        p.tt(rs2[:, :], dst[32:64, :], rpC[32:64, :], ALU.mult, reads=[dst, rpC], writes=[rs2])
        p.tt(rn2[:, :], rs1[:, :], rs2[:, :], ALU.add, reads=[rs1, rs2], writes=[rn2])
        p.vcopy(dst[0:32, :], rn1[:, :], reads=[rn1], writes=[dst])
        p.vcopy(dst[32:64, :], rn2[:, :], reads=[rn2], writes=[dst])

    for b in range(BPC):
        for tb in range(NTB):
            T0 = tb * TBLK
            p.dma("sync", xblk[:, :].rearrange("p (c t) -> p c t", c=CH),
                  xfm[b].rearrange("(c p) t -> p c t", p=128)[:, :, T0:T0 + TBLK],
                  writes=[xblk])
            for c in range(CH):
                p.mm(ps_r[:, :], onesfc[:, :], xblk[:, c * TBLK:(c + 1) * TBLK],
                     reads=[onesfc, xblk], writes=[ps_r],
                     start=(c == 0), stop=(c == CH - 1))
            p.ts(stat[:, :], ps_r[0:1, :], 1.0 / D, ALU.mult, reads=[ps_r], writes=[stat])
            for c in range(CH):
                p.act(scr[:, :], xblk[:, c * TBLK:(c + 1) * TBLK], AF.Square,
                      reads=[xblk], writes=[scr])
                p.mm(ps_r[:, :], onesfc[:, :], scr[:, :], reads=[onesfc, scr],
                     writes=[ps_r], start=(c == 0), stop=(c == CH - 1))
            p.ts(stat2[:, :], ps_r[0:1, :], 1.0 / D, ALU.mult, reads=[ps_r], writes=[stat2])
            p.act(stat3[:, :], stat[:, :], AF.Square, reads=[stat], writes=[stat3])
            p.tt(stat2[:, :], stat2[:, :], stat3[:, :], ALU.subtract,
                 reads=[stat3], writes=[stat2])
            p.ts(stat2[:, :], stat2[:, :], 1e-5, ALU.add, reads=[stat2], writes=[stat2])
            p.act(stat3[:, :], stat2[:, :], AF.Sqrt, reads=[stat2], writes=[stat3])
            p.vrecip(stat3[:, :], stat3[:, :], reads=[stat3], writes=[stat3])
            p.tt(stat[:, :], stat[:, :], stat3[:, :], ALU.mult, reads=[stat3], writes=[stat])
            p.ts(stat[:, :], stat[:, :], -1.0, ALU.mult, reads=[stat], writes=[stat])
            p.mm(ps_w[:, :], onesfp[0:1, :], stat3[0:1, :], reads=[onesfp, stat3], writes=[ps_w])
            p.act(scr[:, :], ps_w[:, :], reads=[ps_w], writes=[scr])
            p.mm(ps_w2[:, :], onesfp[0:1, :], stat[0:1, :], reads=[onesfp, stat], writes=[ps_w2])
            p.act(scr2[:, :], ps_w2[:, :], reads=[ps_w2], writes=[scr2])
            for c in range(CH):
                p.tt(scr3[:, :], xblk[:, c * TBLK:(c + 1) * TBLK], scr[:, :],
                     ALU.mult, reads=[xblk, scr], writes=[scr3])
                p.tt(scr3[:, :], scr3[:, :], scr2[:, :], ALU.add,
                     reads=[scr2], writes=[scr3])
                p.dma("sync", dr["uscr"][b, c * 128:(c + 1) * 128, T0:T0 + TBLK],
                      scr3[:, :], reads=[scr3], writes=[dr["uscr"]])
        for w in ("q", "k", "v"):
            p.dma("gpsimd", WB[w][:, :].rearrange("p (c o) -> p c o", c=CH),
                  dr["w" + w].rearrange("(c p) o -> p c o", p=128), writes=[WB[w]])
        rmsT(dr["uscr"], b, s1sb, xrT, 1e-6)
        if debug and b == 0:
            p.dma("gpsimd", dr["d_xr"], xrT[:, :], reads=[xrT])
        p.memset(vh[:, :].rearrange("p (n k) -> p n k", k=65)[:, :, 64:65], 1.0,
                 writes=[vh])
        for h in range(NH):
            o0 = h * HD
            for dst, w, sc_ in ((qT, "q", 0.125), (kT, "k", None)):
                for tb in range(NTB):
                    T0 = tb * TBLK
                    for c in range(CH):
                        p.mm(ps_w[0:64, :], WB[w][:, c * D + o0:c * D + o0 + HD],
                             xrT[:, c * T + T0:c * T + T0 + TBLK],
                             reads=[WB[w], xrT], writes=[ps_w],
                             start=(c == 0), stop=(c == CH - 1))
                    p.act(dst[:, T0:T0 + TBLK], ps_w[0:64, :], AF.Identity,
                          scale=sc_, reads=[ps_w], writes=[dst])
                rope(dst)
            for tt_ in range(NTT):
                t0 = tt_ * 128
                for c in range(CH):
                    p.mm(ps_w[:, 0:64], xrT[:, c * T + t0:c * T + t0 + 128],
                         WB["v"][:, c * D + o0:c * D + o0 + HD],
                         reads=[WB["v"], xrT], writes=[ps_w],
                         start=(c == 0), stop=(c == CH - 1))
                p.act(vh[:, tt_ * 65:tt_ * 65 + 64], ps_w[:, 0:64], AF.Identity,
                      reads=[ps_w], writes=[vh])
            for qi, src_ in enumerate((qT, kT)):
                col = 26 + qi
                for tb in range(NTB):
                    T0 = tb * TBLK
                    p.act(scr2[0:64, :], src_[:, T0:T0 + TBLK], AF.Square,
                          reads=[src_], writes=[scr2])
                    p.mm(ps_r[:, :], ones64[:, :], scr2[0:64, :],
                         reads=[ones64, scr2], writes=[ps_r])
                    p.vred(stat[:, 0:1], ps_r[0:1, :], ALU.max, False,
                           reads=[ps_r], writes=[stat])
                    if tb == 0:
                        p.vcopy(scal[:, col:col + 1], stat[:, 0:1], reads=[stat], writes=[scal])
                    else:
                        p.tt(scal[:, col:col + 1], scal[:, col:col + 1], stat[:, 0:1],
                             ALU.max, reads=[stat], writes=[scal])
            p.tt(scal[:, 28:29], scal[:, 26:27], scal[:, 27:28], ALU.mult,
                 reads=[scal], writes=[scal])
            p.act(scal[:, 29:30], scal[:, 28:29], AF.Sqrt, reads=[scal], writes=[scal])
            p.ts(negm[:, h:h + 1], scal[:, 29:30], -1.0, ALU.mult,
                 reads=[scal], writes=[negm])
            p.mm(ps_w2[:, 8:9], onesfp[0:1, :], negm[0:1, h:h + 1],
                 reads=[onesfp, negm], writes=[ps_w2])
            p.act(negmb[:, h:h + 1], ps_w2[:, 8:9], reads=[ps_w2], writes=[negmb])
            if debug and b == 0 and h == 0:
                p.dma("gpsimd", dr["d_qT"], qT[:, :], reads=[qT])
                p.dma("gpsimd", dr["d_kT"], kT[:, :], reads=[kT])
                p.dma("gpsimd", dr["d_vh"], vh[:, :], reads=[vh])
                p.dma("gpsimd", dr["d_negm"], negm[:, :], reads=[negm])
            p.barrier()
            for j0 in range(NTB):
                T0 = j0 * TBLK
                na = 4 * j0 + 4
                po = ps_o[j0 % 2]
                for a in range(na):
                    s_ = ps_s[a % 2]
                    pe = pt[a % 2]
                    p.mm(s_[:, :], kT[:, a * 128:(a + 1) * 128], qT[:, T0:T0 + TBLK],
                         reads=[kT, qT], writes=[s_])
                    p.act(pe[:, :], s_[:, :], AF.Exp, bias=negmb[:, h:h + 1],
                          reads=[s_, negmb], writes=[pe])
                    r = a - 4 * j0
                    if r >= 0:
                        p.tt(pe[:, :], pe[:, :], cmask[:, r * 512:(r + 1) * 512],
                             ALU.mult, reads=[cmask], writes=[pe])
                    p.mm(po[:, :], vh[:, a * 65:a * 65 + 65], pe[:, :],
                         reads=[vh, pe], writes=[po],
                         start=(a == 0), stop=(a == na - 1))
                p.vrecip(dnh[:, :], po[64:65, :], reads=[po], writes=[dnh])
                p.mm(ps_w[0:64, :], onesfp[0:1, 0:64], dnh[0:1, :],
                     reads=[onesfp, dnh], writes=[ps_w])
                p.act(scr[0:64, :], ps_w[0:64, :], reads=[ps_w], writes=[scr])
                p.tt(ynT[(h % 2) * 64:(h % 2) * 64 + 64,
                         (h // 2) * T + T0:(h // 2) * T + T0 + TBLK],
                     po[0:64, :], scr[0:64, :], ALU.mult,
                     reads=[po, scr], writes=[ynT])
        p.barrier()
        if debug and b == 0:
            p.dma("gpsimd", dr["d_yn"], ynT[:, :], reads=[ynT])
        for co in range(CH):
            for tb in range(NTB):
                T0 = tb * TBLK
                for c in range(CH):
                    p.mm(ps_w2[:, :], WBo[:, c * D + co * 128:c * D + (co + 1) * 128],
                         ynT[:, c * T + T0:c * T + T0 + TBLK],
                         reads=[WBo, ynT], writes=[ps_w2],
                         start=(c == 0), stop=(c == CH - 1))
                p.dma("sync", scr3[:, :], dr["uscr"][b, co * 128:(co + 1) * 128, T0:T0 + TBLK],
                      reads=[dr["uscr"]], writes=[scr3])
                p.tt(scr3[:, :], ps_w2[:, :], scr3[:, :], ALU.add,
                     reads=[ps_w2], writes=[scr3])
                p.dma("sync", dr["hscr"][b, co * 128:(co + 1) * 128, T0:T0 + TBLK],
                      scr3[:, :], reads=[scr3], writes=[dr["hscr"]])

        rmsT(dr["hscr"], b, s2sb, xrT, 1e-6, rrow_out=dr["rrow"])
        for tt_ in range(NTT):
            t0 = tt_ * 128
            p.dma("sync", xblk[:, 0:CH * 128].rearrange("p (c t) -> p c t", c=CH),
                  dr["hscr"][b].rearrange("(c p) t -> p c t", p=128)[:, :, t0:t0 + 128],
                  reads=[dr["hscr"]], writes=[xblk])
            for c in range(CH):
                p.vcopy(rtb[:, c * 128:(c + 1) * 128], xblk[:, c * 128:(c + 1) * 128],
                        reads=[xblk], writes=[vh])
            for c in range(CH):
                p.mm(ps_w[:, 0:8], rtb[:, c * 128:(c + 1) * 128],
                     rw2b[:, c * 8:(c + 1) * 8],
                     reads=[vh, rw2b], writes=[ps_w],
                     start=(c == 0), stop=(c == CH - 1))
            p.dma("sync", gum[:, :], dr["gum"][b, t0:t0 + 128, :], writes=[gum])
            p.dma("sync", rcol[:, :], dr["rrow"][b, 0:1, t0:t0 + 128],
                  reads=[dr["rrow"]], writes=[rcol])
            p.stt(lgt[:, :], ps_w[:, 0:8], rcol[:, 0:1], gum[:, :], ALU.mult, ALU.add,
                  reads=[ps_w, rcol, gum], writes=[lgt])
            p.vred(colA[:, :], lgt[:, :], ALU.max, False, reads=[lgt], writes=[colA])
            p.ts(colA[:, :], colA[:, :], -1.0, ALU.mult, reads=[colA], writes=[colA])
            p.act(prb[:, :], lgt[:, :], AF.Exp, bias=colA[:, 0:1], accum_out=colB[:, 0:1],
                  reads=[lgt, colA], writes=[prb, colB])
            p.vrecip(colA[:, :], colB[:, :], reads=[colB], writes=[colA])
            p.ts(prb[:, :], prb[:, :], colA[:, 0:1], ALU.mult, reads=[colA], writes=[prb])
            p.vred(colA[:, :], prb[:, :], ALU.max, False, reads=[prb], writes=[colA])
            p.ts(lgt[:, :], prb[:, :], colA[:, 0:1], ALU.is_ge, reads=[prb, colA], writes=[lgt])
            p.stt(lgt[:, :], lgt[:, :], -1e30, prb[:, :], ALU.mult, ALU.add,
                  reads=[prb], writes=[lgt])
            p.vred(colB[:, :], lgt[:, :], ALU.max, False, reads=[lgt], writes=[colB])
            p.ts(lgt[:, :], prb[:, :], colB[:, 0:1], ALU.is_ge, reads=[prb, colB], writes=[lgt])
            p.tt(prb[:, :], prb[:, :], lgt[:, :], ALU.mult, reads=[lgt], writes=[prb])
            p.vcopy(prbb[:, :], prb[:, :], reads=[prb], writes=[prbb])
            p.mm(ps_w2[0:8, 0:128], prbb[:, :], identb[:, :], reads=[prbb, identb],
                 writes=[ps_w2])
            p.act(wgT[:, t0:t0 + 128], ps_w2[0:8, 0:128], reads=[ps_w2], writes=[wgT])
        p.barrier()
        if debug and b == 0:
            p.dma("gpsimd", dr["d_wgT"], wgT[:, :], reads=[wgT])
        for e in range(8):
            p.barrier()
            p.dma("gpsimd", WB["q"][:, :].rearrange("p (c o) -> p c o", c=CH),
                  dr["w1"][e].rearrange("(c p) o -> p c o", p=128), writes=[WB["q"]])
            p.dma("gpsimd", WB["k"][:, :].rearrange("p (c o) -> p c o", c=CH),
                  dr["w2"][e].rearrange("(c p) o -> p c o", p=128), writes=[WB["k"]])
            p.dma("gpsimd", WB["v"][:, :].rearrange("p (c o) -> p c o", c=CH),
                  dr["wout"][e].rearrange("(c p) o -> p c o", p=128), writes=[WB["v"]])
            p.dma("gpsimd", b1b, dr["b1"][e:e + 1, :], writes=[vh])
            p.dma("gpsimd", b2b, dr["b2"][e:e + 1, :], writes=[vh])
            for tb in range(NTB):
                T0 = tb * TBLK
                p.mm(ps_w[:, :], e8sb[:, e * 128:(e + 1) * 128], wgT[0:8, T0:T0 + TBLK],
                     reads=[e8sb, wgT], writes=[ps_w])
                p.act(scr2[:, :], ps_w[:, :], reads=[ps_w], writes=[scr2])
                for mh in range(CH):
                    pscur = ps_s[mh % 2]
                    for c in range(CH):
                        p.mm(pscur[:, :], WB["q"][:, c * D + mh * 128:c * D + (mh + 1) * 128],
                             xrT[:, c * T + T0:c * T + T0 + TBLK],
                             reads=[WB["q"], xrT], writes=[pscur],
                             start=(c == 0), stop=False)
                    p.mm(pscur[:, :], b1b[:, mh * 128:(mh + 1) * 128], onesbp[0:1, :],
                         reads=[vh, onesbp], writes=[pscur], start=False, stop=True)
                    p.act(h1s[:, mh * TBLK:(mh + 1) * TBLK], pscur[:, :], AF.Silu,
                          reads=[pscur], writes=[h1s])
                    pscur2 = ps_w
                    for c in range(CH):
                        p.mm(pscur2[:, :],
                             WB["k"][:, c * D + mh * 128:c * D + (mh + 1) * 128],
                             xrT[:, c * T + T0:c * T + T0 + TBLK],
                             reads=[WB["k"], xrT], writes=[pscur2],
                             start=(c == 0), stop=False)
                    p.mm(pscur2[:, :], b2b[:, mh * 128:(mh + 1) * 128], onesbp[32:33, :],
                         reads=[vh, onesbp], writes=[pscur2], start=False, stop=True)
                    p.act(h1s[:, (8 + mh) * TBLK:(9 + mh) * TBLK], pscur2[:, :],
                          reads=[pscur2], writes=[h1s])
                    p.tt(h1s[:, mh * TBLK:(mh + 1) * TBLK],
                         h1s[:, mh * TBLK:(mh + 1) * TBLK],
                         h1s[:, (8 + mh) * TBLK:(9 + mh) * TBLK], ALU.mult, writes=[h1s])
                    p.tt(h1s[:, mh * TBLK:(mh + 1) * TBLK],
                         h1s[:, mh * TBLK:(mh + 1) * TBLK], scr2[:, :], ALU.mult,
                         reads=[scr2], writes=[h1s])
                if debug and b == 0 and e == 0 and tb == 0:
                    p.dma("gpsimd", dr["d_h1"], h1s[:, 0:CH * TBLK], reads=[h1s])
                for do in range(CH):
                    for mh in range(CH):
                        p.mm(ps_w2[:, :], WB["v"][:, mh * D + do * 128:mh * D + (do + 1) * 128],
                             h1s[:, mh * TBLK:(mh + 1) * TBLK],
                             reads=[WB["v"], h1s], writes=[ps_w2],
                             start=(mh == 0), stop=(mh == CH - 1))
                    p.act(scr3[:, :], ps_w2[:, :], reads=[ps_w2], writes=[scr3])
                    if e == 0:
                        p.dma("gpsimd", dr["eoscr"][do * 128:(do + 1) * 128, T0:T0 + TBLK],
                              scr3[:, :], reads=[scr3], writes=[dr["eoscr"]])
                    else:
                        p.do("gpsimd",
                             lambda h_, o=dr["eoscr"][do * 128:(do + 1) * 128, T0:T0 + TBLK],
                             i=scr3[:, :]: h_.dma_start(out=o, in_=i, accum_op=ALU.add),
                             reads=[scr3], writes=[dr["eoscr"]], dmaq="gpsimd")
        p.barrier()
        p.dma("gpsimd", boutsb[:, :], dr["bout"], writes=[boutsb])
        for do in range(CH):
            for tb in range(NTB):
                T0 = tb * TBLK
                p.mm(ps_w[:, :], boutsb[:, do * 128:(do + 1) * 128], wgT[0:8, T0:T0 + TBLK],
                     reads=[boutsb, wgT], writes=[ps_w])
                p.dma("sync", scr[:, :], dr["eoscr"][do * 128:(do + 1) * 128, T0:T0 + TBLK],
                      reads=[dr["eoscr"]], writes=[scr])
                p.dma("sync", scr3[:, :], dr["hscr"][b, do * 128:(do + 1) * 128, T0:T0 + TBLK],
                      reads=[dr["hscr"]], writes=[scr3])
                p.tt(scr[:, :], scr[:, :], scr3[:, :], ALU.add, reads=[scr3], writes=[scr])
                p.tt(scr[:, :], scr[:, :], ps_w[:, :], ALU.add, reads=[ps_w], writes=[scr])
                p.dma("sync", scr3[:, :], xfm[b, do * 128:(do + 1) * 128, T0:T0 + TBLK],
                      writes=[scr3])
                p.tt(scr[:, :], scr[:, :], scr3[:, :], ALU.add, reads=[scr3], writes=[scr])
                p.dma("sync", xofm[b, do * 128:(do + 1) * 128, T0:T0 + TBLK], scr[:, :],
                      reads=[scr])


def kernel(**inputs):
    consts = host_consts()
    x = np.asarray(inputs["L0_attn_x"], dtype=np.float32)
    x_fm = np.ascontiguousarray(x.transpose(0, 2, 1))
    shards = [np.ascontiguousarray(x_fm[c * BPC:(c + 1) * BPC]) for c in range(NCORE)]
    cm = np.ascontiguousarray(consts["cmask"].astype(np.float32))
    e8 = np.zeros((8, D), np.float32)
    for e in range(8):
        e8[e, e * 128:(e + 1) * 128] = 1.0
    ident = np.eye(128, dtype=np.float32)
    cores = list(range(NCORE))

    nc = build_l0a()
    in_maps = []
    for c in cores:
        m = {"x": shards[c], "cmask": cm}
        for wn in ("q", "k", "v", "o"):
            m["w" + wn] = inputs["L0_attn_W" + wn]
            m["b" + wn] = np.asarray(inputs["L0_attn_b" + wn]).reshape(1, D)
        in_maps.append(m)
    res = run_bass_kernel_spmd(nc, in_maps, core_ids=cores)
    shards = [np.asarray(r["xo"]) for r in res.results]

    nc = build_l0f()
    gum = np.ascontiguousarray(consts["gumbel"].transpose(0, 2, 1))
    in_maps = []
    for c in cores:
        m = {"x": shards[c],
             "wq": inputs["L0_ffn_wq"], "wk": inputs["L0_ffn_wk"],
             "wv": inputs["L0_ffn_wv"], "wo": inputs["L0_ffn_wo"],
             "scale1": np.ascontiguousarray(np.asarray(inputs["L0_ffn_scale1"]).reshape(CH, 128).T),
             "scale2": np.ascontiguousarray(np.asarray(inputs["L0_ffn_scale2"]).reshape(CH, 128).T),
             "rw": inputs["L0_ffn_router_w"],
             "w1": inputs["L0_ffn_w1"], "w2": inputs["L0_ffn_w2"],
             "wout": inputs["L0_ffn_wout"],
             "b1": inputs["L0_ffn_b1"], "b2": inputs["L0_ffn_b2"],
             "bout": inputs["L0_ffn_bout"],
             "gum": np.ascontiguousarray(gum[c * BPC:(c + 1) * BPC]),
             "cmask": cm, "rsin": consts["rope_sin"], "rcos": consts["rope_cos"],
             "e8": e8, "ident": ident}
        in_maps.append(m)
    res = run_bass_kernel_spmd(nc, in_maps, core_ids=cores)
    shards = [np.asarray(r["xo"]) for r in res.results]

    nc = build_l1a()
    in_maps = []
    for c in cores:
        in_maps.append({"x": shards[c],
                        "wq": inputs["L1_attn_Wq"], "wk": inputs["L1_attn_Wk"],
                        "wv": inputs["L1_attn_Wv"], "wu": inputs["L1_attn_Wu"],
                        "bu": np.asarray(inputs["L1_attn_bu"]).reshape(1, D)})
    res = run_bass_kernel_spmd(nc, in_maps, core_ids=cores)
    shards = [np.asarray(r["xo"]) for r in res.results]

    nc = build_l1f()
    cw = np.ascontiguousarray(np.asarray(inputs["L1_ffn_conv_w"]).reshape(4096, 3))
    cb = np.ascontiguousarray(np.asarray(inputs["L1_ffn_conv_b"]).reshape(32, 128).T)
    in_maps = []
    for c in cores:
        in_maps.append({"x": shards[c], "w1": inputs["L1_ffn_w1"],
                        "w2": inputs["L1_ffn_w2"], "cw": cw, "cb": cb})
    res = run_bass_kernel_spmd(nc, in_maps, core_ids=cores)
    shards = [np.asarray(r["xo"]) for r in res.results]

    out_fm = np.concatenate(shards, axis=0)
    return np.ascontiguousarray(out_fm.transpose(0, 2, 1))
